# version 06
# baseline (speedup 1.0000x reference)
import numpy as np
import ml_dtypes
from contextlib import ExitStack
import concourse.bass as bass
import concourse.mybir as mybir
from concourse.bass_utils import run_bass_kernel_spmd

F32 = mybir.dt.float32
BF16 = mybir.dt.bfloat16
AF = mybir.ActivationFunctionType
ALU = mybir.AluOpType
AX = mybir.AxisListType

NCORES = 8
D = 1024
T = 1024
NCHUNK = 8
NTT = 8
NPAIR = 8
NKT = 8
H, NHC = 2048, 16
E = 8
EPS = 1e-5


class Prog:
    ENGS = ["sync", "gpsimd", "tensor", "vector", "scalar"]

    def __init__(self, nc, ctx):
        self.nc = nc
        self.sems = {e: ctx.enter_context(nc.semaphore("S" + e)) for e in self.ENGS}
        self.cnt = {e: 0 for e in self.ENGS}
        self.ops = {e: [] for e in self.ENGS}

    def emit(self, e, fn, waits=(), inc=0):
        ws = tuple((we, wv) for (we, wv) in waits if we is not None and wv > 0)
        self.ops[e].append((fn, ws, inc))
        if inc:
            self.cnt[e] += inc
        return (e, self.cnt[e])

    def tok(self, e):
        return (e, self.cnt[e])

    def build(self):
        with self.nc.Block() as block:
            for e in self.ENGS:
                dec = getattr(block, e)

                def _section(eng, _e=e):
                    seen = {}
                    for (fn, ws, inc) in self.ops[_e]:
                        for (we, wv) in ws:
                            if seen.get(we, 0) >= wv:
                                continue
                            eng.wait_ge(self.sems[we], wv)
                            seen[we] = wv
                        ins = fn(eng)
                        if inc:
                            ins.then_inc(self.sems[_e], inc)
                dec(_section)


def build(n_chunks=NCHUNK, dbg=None, trunc=None):
    nc = bass.Bass("TRN2", target_bir_lowering=False)
    dram = lambda name, shape, dt, kind: nc.dram_tensor(name, shape, dt, kind=kind).ap()

    x_h = dram("x", [n_chunks, T, D], F32, "ExternalInput")
    out_h = dram("out", [n_chunks, T, D], F32, "ExternalOutput")
    pt_l = dram("pt_l", [D, D], F32, "ExternalInput")
    pt_g = dram("pt_g", [D, D], F32, "ExternalInput")
    wint_l = dram("wint_l", [D, 3 * D], F32, "ExternalInput")
    wint_g = dram("wint_g", [D, 3 * D], F32, "ExternalInput")
    wot_l = dram("wot_l", [D, D], F32, "ExternalInput")
    wot_g = dram("wot_g", [D, D], F32, "ExternalInput")
    w1_h = dram("w1", [E, D, H], F32, "ExternalInput")
    w2_h = dram("w2", [E, H, D], F32, "ExternalInput")
    b2_h = dram("b2", [E, D], F32, "ExternalInput")
    bqk_l_h = dram("bqk_l", [128, 16], F32, "ExternalInput")
    bqk_g_h = dram("bqk_g", [128, 16], F32, "ExternalInput")
    bv_l_h = dram("bv_l", [1, D], F32, "ExternalInput")
    bv_g_h = dram("bv_g", [1, D], F32, "ExternalInput")
    bout_l_h = dram("bout_l", [1, D], F32, "ExternalInput")
    bout_g_h = dram("bout_g", [1, D], F32, "ExternalInput")
    gam_h = dram("gam", [128, D], F32, "ExternalInput")
    bet_h = dram("bet", [128, D], F32, "ExternalInput")
    fus_h = dram("fus", [128, 2], F32, "ExternalInput")
    wg_h = dram("wgg", [128, 64], F32, "ExternalInput")
    bg_h = dram("bgg", [1, E], F32, "ExternalInput")
    b1c_h = dram("b1c", [128, 128], F32, "ExternalInput")
    ident_h = dram("identc", [128, 128], BF16, "ExternalInput")
    selt_h = dram("seltc", [128, 128], BF16, "ExternalInput")
    selb_h = dram("selbc", [128, 128], BF16, "ExternalInput")
    iota_h = dram("iotac", [128, 24], F32, "ExternalInput")
    onesb_h = dram("onesbc", [1, 640], BF16, "ExternalInput")
    tril8b_h = dram("tril8bc", [128, 128], BF16, "ExternalInput")
    mu1_h = dram("mu1c", [128, 128], BF16, "ExternalInput")
    mv1_h = dram("mv1c", [128, 128], BF16, "ExternalInput")
    mu2_h = dram("mu2c", [128, 128], BF16, "ExternalInput")
    mv2_h = dram("mv2c", [128, 128], BF16, "ExternalInput")

    dbg_h = dram("dbgout", [128, 16384], F32, "ExternalOutput") if dbg else None
    wints_l = dram("wints_l", [D, 3 * D], BF16, "Internal")
    wints_g = dram("wints_g", [D, 3 * D], BF16, "Internal")
    m_l = dram("m_l", [D, 3 * D], BF16, "Internal")
    m_g = dram("m_g", [D, 3 * D], BF16, "Internal")
    wos_l = dram("wos_l", [D, D], BF16, "Internal")
    wos_g = dram("wos_g", [D, D], BF16, "Internal")
    w1s = dram("w1s", [E, D, H], BF16, "Internal")
    w2s = dram("w2s", [E, H + 1, D], BF16, "Internal")

    ctx = ExitStack()
    sb = lambda name, shape, dt: ctx.enter_context(nc.sbuf_tensor(name, shape, dt))
    psum = lambda name, shape, dt: ctx.enter_context(nc.psum_tensor(name, shape, dt))

    xres = sb("xres", [128, NTT * D], F32)
    lnfm = sb("lnfm", [128, 8 * T], BF16)
    qk_h = sb("qk_h", [128, 16384], BF16)
    vt = sb("vt", [128, NTT * D], BF16)
    fused = sb("fusedb", [128, 8 * T], BF16)
    expt = sb("expt", [128, 2048], BF16)
    wbuf = sb("wbuf", [128, 32768], BF16)
    scr = sb("scr", [128, 1024], F32)
    gst = sb("gst", [128, 256], F32)
    stt = sb("stt", [128, 64], F32)
    gam = sb("gamS", [128, D], F32)
    bet = sb("betS", [128, D], F32)
    fusw = sb("fuswS", [128, 4], F32)
    bqkl = sb("bqklS", [128, 16], F32)
    bqkg = sb("bqkgS", [128, 16], F32)
    b1c = sb("b1cS", [128, 128], F32)
    wg_sb = sb("wgS", [128, 64], BF16)
    iot = sb("iotS", [128, 24], F32)
    ident = sb("identS", [128, 128], BF16)
    selt = sb("seltS", [128, 128], BF16)
    selb = sb("selbS", [128, 128], BF16)
    tril8b = sb("tril8bS", [128, 128], BF16)
    mu1 = sb("mu1S", [128, 128], BF16)
    mv1 = sb("mv1S", [128, 128], BF16)
    mu2 = sb("mu2S", [128, 128], BF16)
    mv2 = sb("mv2S", [128, 128], BF16)
    rowc = sb("rowcS", [128, 4864], BF16)
    rows2 = sb("rows2S", [128, 1024], BF16)
    R_BVL, R_BVG, R_BOL, R_BOG, R_BG, R_ON512, R_ONK1 = 0, 1024, 2048, 3072, 4096, 4104, 4616

    fb = [psum("f%d" % i, [128, 512], F32) for i in range(6)]
    ptrs = [psum("ptrb%d" % i, [128, 1024], BF16) for i in range(2)]

    P = Prog(nc, ctx)
    last = {}

    def tk(name):
        return last.get(name, (None, 0))

    def ones512():
        return rowc[0:1, R_ON512:R_ON512 + 512]

    def onesk1():
        return rowc[0:1, R_ONK1:R_ONK1 + 128]

    def dbg_tap(name, buf, cols, chunk):
        if dbg is None or dbg != name or chunk != 0:
            return
        wl = [P.tok(e) for e in Prog.ENGS]
        t_tap = P.emit("gpsimd", lambda e: e.dma_start(out=dbg_h[:, 0:cols], in_=buf[:, 0:cols]),
                       waits=wl, inc=16)
        for e2 in Prog.ENGS:
            P.emit(e2, lambda e: e.nop(), waits=[t_tap])

    for src, dst in [(gam_h, gam), (bet_h, bet), (bqk_l_h, bqkl), (bqk_g_h, bqkg),
                     (b1c_h, b1c), (iota_h, iot), (ident_h, ident),
                     (selt_h, selt), (selb_h, selb), (tril8b_h, tril8b),
                     (mu1_h, mu1), (mv1_h, mv1), (mu2_h, mu2), (mv2_h, mv2)]:
        P.emit("sync", lambda e, s=src, d=dst: e.dma_start(out=d[:], in_=s), inc=16)
    P.emit("sync", lambda e: e.dma_start(out=fusw[:, 0:2], in_=fus_h), inc=16)
    P.emit("sync", lambda e: e.dma_start(out=rowc[0:1, R_ON512:R_ONK1 + 128], in_=onesb_h), inc=16)
    t_const = P.tok("sync")

    P.emit("gpsimd", lambda e: e.dma_start(out=wg_sb[:], in_=wg_h), inc=16)
    for src, dst in [(bv_l_h, rowc[0:1, R_BVL:R_BVL + D]), (bv_g_h, rowc[0:1, R_BVG:R_BVG + D]),
                     (bout_l_h, rowc[0:1, R_BOL:R_BOL + D]), (bout_g_h, rowc[0:1, R_BOG:R_BOG + D]),
                     (bg_h, rowc[0:1, R_BG:R_BG + E])]:
        P.emit("gpsimd", lambda e, s=src, d=dst: e.dma_start(out=d, in_=s), inc=16)
    t_smallcast = P.tok("gpsimd")

    P.emit("gpsimd", lambda e: e.dma_start(out=wints_l, in_=wint_l), inc=16)
    t_wints = {0: P.tok("gpsimd")}
    P.emit("gpsimd", lambda e: e.dma_start(out=wints_g, in_=wint_g), inc=16)
    t_wints[1] = P.tok("gpsimd")
    P.emit("gpsimd", lambda e: e.dma_start(out=wos_l, in_=wot_l), inc=16)
    P.emit("gpsimd", lambda e: e.dma_start(out=wos_g, in_=wot_g), inc=16)
    t_wos = P.tok("gpsimd")
    t_w1c = [None] * E
    t_w2c = [None] * E
    for e_ in range(E):
        P.emit("gpsimd", lambda e, i=e_: e.dma_start(out=w1s[i], in_=w1_h[i]), inc=16)
        t_w1c[e_] = P.tok("gpsimd")
        P.emit("gpsimd", lambda e, i=e_: e.dma_start(out=w2s[i, 0:H, :], in_=w2_h[i]), inc=16)
        P.emit("gpsimd", lambda e, i=e_: e.dma_start(
            out=w2s[i, H:H + 1, :], in_=b2_h[i:i + 1, :]), inc=16)
        t_w2c[e_] = P.tok("gpsimd")

    P.emit("scalar", lambda e: e.activation(stt[:, 0:2], fusw[:, 0:2], AF.Exp),
           waits=[t_const], inc=1)
    t_fx = P.tok("scalar")
    P.emit("vector", lambda e: e.tensor_reduce(stt[:, 2:3], stt[:, 0:2], AX.X, ALU.add),
           waits=[t_fx], inc=1)
    P.emit("vector", lambda e: e.reciprocal(stt[:, 3:4], stt[:, 2:3]),
           waits=[P.tok("vector")], inc=1)
    P.emit("vector", lambda e: e.tensor_single_scalar(fusw[:, 2:4], stt[:, 0:2], stt[:, 3:4], ALU.mult),
           waits=[P.tok("vector")], inc=1)
    t_fusw = P.tok("vector")
    w0col = fusw[:, 2:3]
    w1col = fusw[:, 3:4]

    t_mready = {}
    for which, pt_src, wints_s, m_dst in [(0, pt_l, wints_l, m_l), (1, pt_g, wints_g, m_g)]:
        P.emit("gpsimd", lambda e, s=pt_src: e.dma_start(
            out=wbuf[:, 0:8192].rearrange("p (n k) -> p n k", k=1024),
            in_=s.rearrange("(n p) k -> p n k", p=128)),
            waits=[tk("wb1")], inc=16)
        t_pt = P.tok("gpsimd")
        P.emit("sync", lambda e, s=wints_s: e.dma_start(
            out=wbuf[:, 8192:32768].rearrange("p (n k) -> p n k", k=3072),
            in_=s.rearrange("(n p) k -> p n k", p=128)),
            waits=[t_wints[which], tk("wb2"), tk("wb3")], inc=16)
        t_wi = P.tok("sync")
        for ac in range(8):
            slot = ac % 2
            for fh in range(6):
                pb = fb[fh % 4]
                pbn = "f%d" % (fh % 4)
                for bc in range(8):
                    P.emit("tensor", lambda e, pb=pb, bc=bc, ac=ac, fh=fh: e.matmul(
                        pb[:], wbuf[:, bc * 1024 + ac * 128: bc * 1024 + ac * 128 + 128],
                        wbuf[:, 8192 + bc * 3072 + fh * 512: 8192 + bc * 3072 + fh * 512 + 512],
                        start=(bc == 0), stop=(bc == 7)),
                        waits=[t_pt, t_wi, tk(pbn)] if bc == 0 else (),
                        inc=(1 if bc == 7 else 0))
                t_mm = P.tok("tensor")
                P.emit("scalar", lambda e, pb=pb, slot=slot, fh=fh: e.activation(
                    vt[:, slot * 3072 + fh * 512: slot * 3072 + fh * 512 + 512], pb[:], AF.Copy),
                    waits=[t_mm, tk("vts%d" % slot)], inc=1)
                last[pbn] = P.tok("scalar")
            t_ev = P.tok("scalar")
            P.emit("sync", lambda e, ac=ac, slot=slot, d=m_dst: e.dma_start(
                out=d[ac * 128:(ac + 1) * 128, :], in_=vt[:, slot * 3072:(slot + 1) * 3072]),
                waits=[t_ev], inc=16)
            last["vts%d" % slot] = P.tok("sync")
        last["wb1"] = P.tok("tensor")
        last["wb2"] = P.tok("tensor")
        last["wb3"] = P.tok("tensor")
        last["vt"] = P.tok("sync")
        t_mready[which] = P.tok("sync")


    def ln_stats(src_tile, c0, src_tok, sqslot):
        sq = qk_h[:, sqslot * 1024:(sqslot + 1) * 1024]
        P.emit("scalar", lambda e: e.activation(sq, src_tile, AF.Square),
               waits=[src_tok, tk("qk"), tk("lnsq%d" % sqslot)], inc=1)
        t_sq = P.tok("scalar")
        P.emit("vector", lambda e: e.tensor_reduce(stt[:, c0:c0 + 1], src_tile, AX.X, ALU.add),
               waits=[src_tok, tk("sttc%d" % c0)], inc=1)
        P.emit("vector", lambda e: e.tensor_reduce(stt[:, c0 + 1:c0 + 2], sq, AX.X, ALU.add),
               waits=[t_sq], inc=1)
        last["lnsq%d" % sqslot] = P.tok("vector")
        P.emit("vector", lambda e: e.tensor_single_scalar(
            stt[:, c0 + 2:c0 + 3], stt[:, c0:c0 + 1], 1.0 / D, ALU.mult),
            waits=[P.tok("vector")], inc=1)
        P.emit("vector", lambda e: e.tensor_tensor(
            stt[:, c0 + 5:c0 + 6], stt[:, c0 + 2:c0 + 3], stt[:, c0 + 2:c0 + 3], ALU.mult),
            waits=[P.tok("vector")], inc=1)
        P.emit("vector", lambda e: e.tensor_single_scalar(
            stt[:, c0 + 6:c0 + 7], stt[:, c0 + 1:c0 + 2], 1.0 / D, ALU.mult),
            waits=[P.tok("vector")], inc=1)
        P.emit("vector", lambda e: e.tensor_tensor(
            stt[:, c0 + 3:c0 + 4], stt[:, c0 + 6:c0 + 7], stt[:, c0 + 5:c0 + 6], ALU.subtract),
            waits=[P.tok("vector")], inc=1)
        t_var = P.tok("vector")
        P.emit("scalar", lambda e: e.activation(stt[:, c0 + 5:c0 + 6], stt[:, c0 + 3:c0 + 4],
                                                AF.Ln, bias=iot[:, 16:17]), waits=[t_var], inc=1)
        P.emit("scalar", lambda e: e.activation(stt[:, c0 + 4:c0 + 5], stt[:, c0 + 5:c0 + 6],
                                                AF.Exp, scale=-0.5),
               waits=[P.tok("scalar")], inc=1)
        last["sttc%d" % c0] = P.tok("scalar")
        return P.tok("scalar")

    def tile_transpose(tt, t_ts):
        pb = ptrs[tt % 2]
        pn = "ptr%d" % (tt % 2)
        t_prev_ev = tk(pn)
        for j in range(8):
            P.emit("tensor", lambda e, i=tt, j=j, pb=pb: e.transpose(
                pb[:, j * 128:(j + 1) * 128],
                vt[:, i * 1024 + j * 128: i * 1024 + j * 128 + 128], ident[:]),
                waits=[t_ts, t_prev_ev] if j == 0 else (), inc=(1 if j == 7 else 0))
        t_tr = P.tok("tensor")
        dst = lnfm[:].rearrange("p (j k) -> p j k", k=1024)[:, :, tt * 128:(tt + 1) * 128]
        srcv = pb[:].rearrange("p (j k) -> p j k", k=128)
        P.emit("vector", lambda e, dst=dst, srcv=srcv: e.tensor_copy(dst, srcv),
               waits=[t_tr, tk("lnfm")], inc=1)
        last[pn] = P.tok("vector")

    def emit_sums_av(i, h01, kt, t_e, sm, smn, ub, ubn, sel, hh, ec):
        P.emit("tensor", lambda e, ub=ub, hh=hh, kt=kt, ec=ec, h01=h01: e.matmul(
            ub[h01 * 64:h01 * 64 + 64, :],
            vt[:, kt * 1024 + hh * 64: kt * 1024 + hh * 64 + 64],
            expt[:, ec:ec + 512],
            start=(kt == 0), stop=(kt == NKT - 1)),
            waits=([t_e, tk("vready"), tk(ubn)] if (h01 == 0 and kt == 0) else [t_e]),
            inc=0)
        P.emit("tensor", lambda e, sm=sm, sel=sel, ec=ec, h01=h01, kt=kt: e.matmul(
            sm[:], sel[:], expt[:, ec:ec + 512],
            start=(h01 == 0 and kt == 0), stop=(h01 == 1 and kt == NKT - 1)),
            waits=([tk(smn)] if (h01 == 0 and kt == 0) else ()),
            inc=(1 if (h01 == 1 and kt == NKT - 1) else 0))

    class _Trunc(Exception):
        pass

    def maybe_trunc(point):
        if trunc is not None and trunc == point:
            t_all = [P.tok(e) for e in Prog.ENGS]
            P.emit("sync", lambda e: e.dma_start(
                out=out_h[0].rearrange("(a p) d -> p a d", p=128),
                in_=xres[:].rearrange("p (a d) -> p a d", d=D)),
                waits=t_all, inc=16)
            raise _Trunc()

    def emit_chunk(c):
        P.emit("sync", lambda e: e.dma_start(
            out=xres[:].rearrange("p (a d) -> p a d", d=D),
            in_=x_h[c].rearrange("(a p) d -> p a d", p=128)),
            waits=[tk("xres")], inc=16)
        t_x = P.tok("sync")

        for tt in range(NTT):
            xt = xres[:, tt * D:(tt + 1) * D]
            c0 = (tt % 4) * 8
            t_rstd = ln_stats(xt, c0, t_x, tt)
            P.emit("vector", lambda e, xt=xt, tt=tt, c0=c0: e.tensor_scalar(
                vt[:, tt * D:(tt + 1) * D], xt, stt[:, c0 + 2:c0 + 3], stt[:, c0 + 4:c0 + 5],
                ALU.subtract, ALU.mult),
                waits=[t_rstd, P.tok("vector"), tk("vt")], inc=1)
            tile_transpose(tt, P.tok("vector"))
        t_lnfm = P.tok("vector")
        last["lnfm"] = t_lnfm
        last["vt"] = P.tok("tensor")
        last["qk"] = P.tok("vector")
        dbg_tap("ln0fm", lnfm, 8192, c)
        maybe_trunc("s2")

        def load_m(which):
            m_src = m_l if which == 0 else m_g
            P.emit("sync", lambda e, s=m_src: e.dma_start(
                out=wbuf[:, 0:24576].rearrange("p (n k) -> p n k", k=3072),
                in_=s.rearrange("(n p) k -> p n k", p=128)),
                waits=[tk("wb1"), tk("wb2"), t_mready[which]], inc=16)
            return P.tok("sync")

        def load_wo(which):
            wo_src = wos_l if which == 0 else wos_g
            P.emit("sync", lambda e, s=wo_src: e.dma_start(
                out=wbuf[:, 24576:32768].rearrange("p (n k) -> p n k", k=1024),
                in_=s.rearrange("(n p) k -> p n k", p=128)),
                waits=[t_wos, tk("wb3")], inc=16)
            return P.tok("sync")

        def qkv(which, t_wload):
            bqk = bqkl if which == 0 else bqkg
            rbv = R_BVL if which == 0 else R_BVG
            gi = 0
            for part in range(2):
                for fc in range(8):
                    b0 = (gi % 2) * 2
                    pa, pc = fb[b0], fb[b0 + 1]
                    pan, pcn = "f%d" % b0, "f%d" % (b0 + 1)
                    wl = ([t_wload, t_lnfm, tk(pan), tk(pcn)] if gi < 2
                          else [tk(pan), tk(pcn)])
                    for dc in range(8):
                        P.emit("tensor", lambda e, pa=pa, dc=dc, fc=fc, part=part: e.matmul(
                            pa[:],
                            wbuf[:, dc * 3072 + part * 1024 + fc * 128: dc * 3072 + part * 1024 + fc * 128 + 128],
                            lnfm[:, dc * 1024: dc * 1024 + 512],
                            start=(dc == 0), stop=(dc == 7)),
                            waits=wl if dc == 0 else ())
                        P.emit("tensor", lambda e, pc=pc, dc=dc, fc=fc, part=part: e.matmul(
                            pc[:],
                            wbuf[:, dc * 3072 + part * 1024 + fc * 128: dc * 3072 + part * 1024 + fc * 128 + 128],
                            lnfm[:, dc * 1024 + 512: dc * 1024 + 1024],
                            start=(dc == 0), stop=(dc == 7)),
                            inc=(1 if dc == 7 else 0))
                    t_mm = P.tok("tensor")
                    dst_c = part * 8192 + fc * 1024
                    P.emit("scalar", lambda e, pa=pa, dst_c=dst_c, bqk=bqk, part=part, fc=fc: e.activation(
                        qk_h[:, dst_c:dst_c + 512], pa[:], AF.Identity,
                        bias=bqk[:, part * 8 + fc:part * 8 + fc + 1]),
                        waits=[t_mm, tk("qk")], inc=1)
                    last[pan] = P.tok("scalar")
                    P.emit("scalar", lambda e, pc=pc, dst_c=dst_c, bqk=bqk, part=part, fc=fc: e.activation(
                        qk_h[:, dst_c + 512:dst_c + 1024], pc[:], AF.Identity,
                        bias=bqk[:, part * 8 + fc:part * 8 + fc + 1]),
                        inc=1)
                    last[pcn] = P.tok("scalar")
                    gi += 1
            for tt in range(8):
                pa, pc = fb[4], fb[5]
                wl = [t_wload, t_lnfm, tk("f4"), tk("f5"), t_smallcast]
                for dc in range(8):
                    P.emit("tensor", lambda e, pa=pa, dc=dc, tt=tt: e.matmul(
                        pa[:],
                        lnfm[:, dc * 1024 + tt * 128: dc * 1024 + tt * 128 + 128],
                        wbuf[:, dc * 3072 + 2048: dc * 3072 + 2048 + 512],
                        start=(dc == 0), stop=False),
                        waits=wl if dc == 0 else ())
                    P.emit("tensor", lambda e, pc=pc, dc=dc, tt=tt: e.matmul(
                        pc[:],
                        lnfm[:, dc * 1024 + tt * 128: dc * 1024 + tt * 128 + 128],
                        wbuf[:, dc * 3072 + 2560: dc * 3072 + 3072],
                        start=(dc == 0), stop=False))
                P.emit("tensor", lambda e, pa=pa, rbv=rbv: e.matmul(
                    pa[:], onesk1(), rowc[0:1, rbv: rbv + 512],
                    start=False, stop=True))
                P.emit("tensor", lambda e, pc=pc, rbv=rbv: e.matmul(
                    pc[:], onesk1(), rowc[0:1, rbv + 512: rbv + 1024],
                    start=False, stop=True), inc=1)
                t_mm = P.tok("tensor")
                P.emit("vector", lambda e, pa=pa, tt=tt: e.tensor_copy(
                    vt[:, tt * 1024: tt * 1024 + 512], pa[:]),
                    waits=[t_mm, tk("vt")], inc=1)
                last["f4"] = P.tok("vector")
                P.emit("vector", lambda e, pc=pc, tt=tt: e.tensor_copy(
                    vt[:, tt * 1024 + 512: tt * 1024 + 1024], pc[:]),
                    inc=1)
                last["f5"] = P.tok("vector")
            last["qkready"] = P.tok("scalar")
            last["vready"] = P.tok("vector")
            last["wb1"] = P.tok("tensor")
            last["wb2"] = P.tok("tensor")

        def attn_out_proj(which):
            rbo = R_BOL if which == 0 else R_BOG
            gi = 0
            for dc in range(8):
                b0 = (gi % 2) * 2
                pa, pc = fb[b0], fb[b0 + 1]
                pan, pcn = "f%d" % b0, "f%d" % (b0 + 1)
                wl = ([tk("oready"), tk(pan), tk(pcn)] if gi < 2 else [tk(pan), tk(pcn)])
                gi += 1
                for fc in range(8):
                    P.emit("tensor", lambda e, pa=pa, fc=fc, dc=dc: e.matmul(
                        pa[:],
                        wbuf[:, 24576 + fc * 1024 + dc * 128: 24576 + fc * 1024 + dc * 128 + 128],
                        qk_h[:, fc * 1024: fc * 1024 + 512],
                        start=(fc == 0), stop=False),
                        waits=wl if fc == 0 else ())
                    P.emit("tensor", lambda e, pc=pc, fc=fc, dc=dc: e.matmul(
                        pc[:],
                        wbuf[:, 24576 + fc * 1024 + dc * 128: 24576 + fc * 1024 + dc * 128 + 128],
                        qk_h[:, fc * 1024 + 512: fc * 1024 + 1024],
                        start=(fc == 0), stop=False))
                P.emit("tensor", lambda e, pa=pa, dc=dc, rbo=rbo: e.matmul(
                    pa[:], rowc[0:1, rbo + dc * 128: rbo + dc * 128 + 128], ones512(),
                    start=False, stop=True))
                P.emit("tensor", lambda e, pc=pc, dc=dc, rbo=rbo: e.matmul(
                    pc[:], rowc[0:1, rbo + dc * 128: rbo + dc * 128 + 128], ones512(),
                    start=False, stop=True), inc=1)
                t_mm = P.tok("tensor")
                for th, pb2, pbn2 in [(0, pa, pan), (1, pc, pcn)]:
                    dst = fused[:, dc * 1024 + th * 512: dc * 1024 + th * 512 + 512]
                    if which == 0:
                        P.emit("vector", lambda e, pb2=pb2, dst=dst: e.tensor_single_scalar(
                            dst, pb2[:], w0col, ALU.mult),
                            waits=([t_mm, t_fusw, tk("fused")] if th == 0 else ()), inc=1)
                    else:
                        P.emit("vector", lambda e, pb2=pb2, dst=dst: e.scalar_tensor_tensor(
                            dst, pb2[:], w1col, dst, ALU.mult, ALU.add),
                            waits=([t_mm, t_fusw, tk("fusedacc")] if th == 0 else ()), inc=1)
                    last[pbn2] = P.tok("vector")
            last["fusedacc"] = P.tok("vector")
            last["wb3"] = P.tok("tensor")

        t_wl_m = load_m(0)
        load_wo(0)
        qkv(0, t_wl_m)
        dbg_tap("qk_l", qk_h, 16384, c)
        dbg_tap("vt_l", vt, 8192, c)
        t_wl_mg = load_m(1)
        maybe_trunc("qkv0")

        units = [(w, pr) for w in range(8) for pr in range(NPAIR)]

        def l_scores(i):
            w, pr = units[i]
            ec = (i % 4) * 512
            P.emit("tensor", lambda e, pr=pr, w=w: e.matmul(
                fb[0][:, 0:128],
                qk_h[0:64, 8192 + pr * 1024 + w * 128: 8192 + pr * 1024 + w * 128 + 128],
                qk_h[0:64, pr * 1024 + w * 128: pr * 1024 + w * 128 + 128],
                start=True, stop=False),
                waits=[tk("qkready"), tk("f0")])
            P.emit("tensor", lambda e: e.matmul(
                fb[0][:, 0:128], ident[:], tril8b[:], start=False, stop=True), inc=1)
            t_s0 = P.tok("tensor")
            P.emit("tensor", lambda e, pr=pr, w=w: e.matmul(
                fb[1][:, 0:128],
                qk_h[64:128, 8192 + pr * 1024 + w * 128: 8192 + pr * 1024 + w * 128 + 128],
                qk_h[64:128, pr * 1024 + w * 128: pr * 1024 + w * 128 + 128],
                start=True, stop=False),
                waits=[tk("f1")])
            P.emit("tensor", lambda e: e.matmul(
                fb[1][:, 0:128], mu1[64:128, :], mv1[64:128, :], start=False, stop=False))
            P.emit("tensor", lambda e: e.matmul(
                fb[1][:, 0:128], mu2[64:128, :], mv2[64:128, :], start=False, stop=True), inc=1)
            t_s1 = P.tok("tensor")
            P.emit("scalar", lambda e, ec=ec: e.activation(
                expt[:, ec:ec + 128], fb[0][:, 0:128], AF.Exp, scale=0.125),
                waits=[t_s0], inc=1)
            last["f0"] = P.tok("scalar")
            P.emit("scalar", lambda e, ec=ec: e.activation(
                expt[:, ec + 128:ec + 256], fb[1][:, 0:128], AF.Exp, scale=0.125),
                waits=[t_s1], inc=1)
            last["f1"] = P.tok("scalar")
            return P.tok("scalar")

        def l_tail(i, t_e):
            w, pr = units[i]
            sm = fb[2 + i % 2]
            smn = "f%d" % (2 + i % 2)
            ub = fb[4 + i % 2]
            ubn = "f%d" % (4 + i % 2)
            ec = (i % 4) * 512
            P.emit("tensor", lambda e, sm=sm, ec=ec: e.matmul(
                sm[:, 0:128], selt[:], expt[:, ec:ec + 128],
                start=True, stop=False), waits=[t_e, tk(smn)])
            P.emit("tensor", lambda e, sm=sm, ec=ec: e.matmul(
                sm[:, 0:128], selb[:], expt[:, ec + 128:ec + 256],
                start=False, stop=True), inc=1)
            t_sum = P.tok("tensor")
            P.emit("tensor", lambda e, ub=ub, pr=pr, w=w, ec=ec: e.matmul(
                ub[0:64, 0:128],
                vt[:, w * 1024 + (pr * 2) * 64: w * 1024 + (pr * 2) * 64 + 64],
                expt[:, ec:ec + 128],
                start=True, stop=True), waits=[tk("vready"), tk(ubn)])
            P.emit("tensor", lambda e, ub=ub, pr=pr, w=w, ec=ec: e.matmul(
                ub[64:128, 0:128],
                vt[:, w * 1024 + (pr * 2 + 1) * 64: w * 1024 + (pr * 2 + 1) * 64 + 64],
                expt[:, ec + 128:ec + 256],
                start=True, stop=True), inc=1)
            t_av = P.tok("tensor")
            rc = (i % 2) * 512
            P.emit("vector", lambda e, sm=sm, rc=rc: e.reciprocal(
                scr[:, rc:rc + 128], sm[:, 0:128]),
                waits=[t_sum, tk("scrl%d" % (i % 2))], inc=1)
            t_r = P.tok("vector")
            last[smn] = t_r
            oc = pr * 1024 + w * 128
            P.emit("vector", lambda e, ub=ub, oc=oc, rc=rc: e.tensor_tensor(
                qk_h[0:64, oc:oc + 128], ub[0:64, 0:128], scr[0:64, rc:rc + 128], ALU.mult),
                waits=[t_av, t_r], inc=1)
            P.emit("vector", lambda e, ub=ub, oc=oc, rc=rc: e.tensor_tensor(
                qk_h[64:128, oc:oc + 128], ub[64:128, 0:128], scr[64:128, rc:rc + 128], ALU.mult),
                inc=1)
            last[ubn] = P.tok("vector")
            last["scrl%d" % (i % 2)] = P.tok("vector")

        t_es = {}
        for i in range(len(units)):
            t_es[i] = l_scores(i)
            maybe_trunc("lsc%d" % i)
            if i >= 1:
                l_tail(i - 1, t_es[i - 1])
                maybe_trunc("lu%d" % (i - 1))
        l_tail(len(units) - 1, t_es[len(units) - 1])
        last["oready"] = P.tok("vector")
        dbg_tap("oloc", qk_h, 8192, c)
        maybe_trunc("local")
        attn_out_proj(0)
        last["qk"] = P.tok("vector")
        last["vt"] = P.tok("tensor")
        dbg_tap("fusedl", fused, 8192, c)
        maybe_trunc("outproj0")

        load_wo(1)
        qkv(1, t_wl_mg)
        dbg_tap("qk_g", qk_h, 16384, c)
        dbg_tap("vt_g", vt, 8192, c)
        maybe_trunc("qkv1")
        P.emit("sync", lambda e: e.dma_start(
            out=wbuf[:, 0:16384].rearrange("p (n k) -> p n k", k=H),
            in_=w1s[0].rearrange("(n p) k -> p n k", p=128)),
            waits=[t_w1c[0], tk("wb1"), tk("wb2")], inc=16)
        t_w1l = P.tok("sync")

        gunits = [(pr, qb) for pr in range(NPAIR) for qb in range(2)]
        for i, (pr, qb) in enumerate(gunits):
            sm = fb[2 + i % 2]
            smn = "f%d" % (2 + i % 2)
            ub = fb[4 + i % 2]
            ubn = "f%d" % (4 + i % 2)
            pend = None
            for h01 in range(2):
                hh = pr * 2 + h01
                base = 64 * h01
                sel = selt if h01 == 0 else selb
                for kt in range(NKT):
                    sc = fb[kt % 2]
                    scn = "f%d" % (kt % 2)
                    P.emit("tensor", lambda e, sc=sc, base=base, kt=kt, qb=qb, pr=pr: e.matmul(
                        sc[:],
                        qk_h[base:base + 64, 8192 + pr * 1024 + kt * 128: 8192 + pr * 1024 + kt * 128 + 128],
                        qk_h[base:base + 64, pr * 1024 + qb * 512: pr * 1024 + qb * 512 + 512],
                        start=True, stop=True),
                        waits=[tk("qkready"), tk(scn)], inc=1)
                    t_s = P.tok("tensor")
                    dstart = kt * 128 - qb * 512
                    t_m = t_s
                    if 0 <= dstart < 512:
                        P.emit("vector", lambda e, sc=sc, dstart=dstart: e.tensor_tensor(
                            sc[:, dstart:dstart + 128], sc[:, dstart:dstart + 128],
                            tril8b[:], ALU.add), waits=[t_s], inc=1)
                        t_m = P.tok("vector")
                    ec = (h01 * 2 + kt % 2) * 512
                    wl = [t_m]
                    if dstart >= 512:
                        P.emit("scalar", lambda e, sc=sc, ec=ec: e.activation(
                            expt[:, ec:ec + 512], sc[:], AF.Exp, scale=0.125, bias=1.0),
                            waits=wl, inc=1)
                    elif dstart <= 0:
                        P.emit("scalar", lambda e, sc=sc, ec=ec: e.activation(
                            expt[:, ec:ec + 512], sc[:], AF.Exp, scale=0.125),
                            waits=wl, inc=1)
                    else:
                        P.emit("scalar", lambda e, sc=sc, ec=ec, ds=dstart: e.activation(
                            expt[:, ec:ec + ds], sc[:, 0:ds], AF.Exp, scale=0.125, bias=1.0),
                            waits=wl, inc=1)
                        P.emit("scalar", lambda e, sc=sc, ec=ec, ds=dstart: e.activation(
                            expt[:, ec + ds:ec + 512], sc[:, ds:512], AF.Exp, scale=0.125),
                            inc=1)
                    t_e = P.tok("scalar")
                    last[scn] = t_e
                    if pend is not None:
                        emit_sums_av(*pend)
                    pend = (i, h01, kt, t_e, sm, smn, ub, ubn, sel, hh, ec)
            emit_sums_av(*pend)
            t_gsum = P.tok("tensor")
            rc = (i % 2) * 512
            P.emit("vector", lambda e, sm=sm, rc=rc: e.reciprocal(scr[:, rc:rc + 512], sm[:]),
                   waits=[t_gsum, tk("scrg%d" % (i % 2))], inc=1)
            t_r = P.tok("vector")
            last[smn] = t_r
            oc = pr * 1024 + qb * 512
            P.emit("vector", lambda e, ub=ub, oc=oc, rc=rc: e.tensor_tensor(
                qk_h[0:64, oc:oc + 512], ub[0:64, :], scr[0:64, rc:rc + 512], ALU.mult),
                waits=[t_r], inc=1)
            P.emit("vector", lambda e, ub=ub, oc=oc, rc=rc: e.tensor_tensor(
                qk_h[64:128, oc:oc + 512], ub[64:128, :], scr[64:128, rc:rc + 512], ALU.mult),
                inc=1)
            last[ubn] = P.tok("vector")
            last["scrg%d" % (i % 2)] = P.tok("vector")
        last["oready"] = P.tok("vector")
        dbg_tap("oglo", qk_h, 8192, c)
        maybe_trunc("global")
        attn_out_proj(1)
        last["qk"] = P.tok("vector")
        last["vt"] = P.tok("tensor")
        dbg_tap("fusedfm", fused, 8192, c)
        maybe_trunc("outproj1")

        P.emit("sync", lambda e: e.dma_start(
            out=wbuf[:, 16384:32768].rearrange("p (n k) -> p n k", k=D),
            in_=w2s[0, 0:H, :].rearrange("(n p) k -> p n k", p=128)),
            waits=[t_w2c[0], tk("wb2"), tk("wb3")], inc=16)
        P.emit("sync", lambda e: e.dma_start(
            out=rows2[0:1, 0:D], in_=w2s[0, H:H + 1, :]),
            waits=[tk("rows2")], inc=16)
        t_w2l = P.tok("sync")

        t_fa = tk("fusedacc")
        t_tile = {}
        entries = [(i, j) for i in range(8) for j in range(i, 8)]
        groups, cur, nb = [], [], 0
        for en in entries:
            b = 1 if en[0] == en[1] else 2
            if nb + b > 8:
                groups.append(cur)
                cur, nb = [], 0
            cur.append(en)
            nb += b
        groups.append(cur)
        for gi2, grp in enumerate(groups):
            pb = ptrs[gi2 % 2]
            pn = "ptr%d" % (gi2 % 2)
            t_prev = tk(pn)
            blocks = []
            for (i, j) in grp:
                blocks.append((i, j))
                if i != j:
                    blocks.append((j, i))
            for bi, (a, b) in enumerate(blocks):
                P.emit("tensor", lambda e, a=a, b=b, s=bi, pb=pb: e.transpose(
                    pb[:, s * 128:(s + 1) * 128],
                    fused[:, a * 1024 + b * 128: a * 1024 + b * 128 + 128], ident[:]),
                    waits=[t_fa, t_prev] if bi == 0 else (),
                    inc=(1 if bi == len(blocks) - 1 else 0))
            t_t = P.tok("tensor")
            for bi, (a, b) in enumerate(blocks):
                P.emit("vector", lambda e, a=a, b=b, s=bi, pb=pb: e.tensor_copy(
                    fused[:, b * 1024 + a * 128: b * 1024 + a * 128 + 128], pb[:, s * 128:(s + 1) * 128]),
                    waits=[t_t] if bi == 0 else (), inc=(1 if bi == len(blocks) - 1 else 0))
            last[pn] = P.tok("vector")
            for (i, j) in grp:
                if j == 7:
                    t_tile[i] = P.tok("vector")
        t_ftm = P.tok("vector")
        dbg_tap("fusedg", fused, 8192, c)
        maybe_trunc("s7")

        for tt in range(NTT):
            ft = fused[:, tt * D:(tt + 1) * D]
            c0 = (tt % 4) * 8
            t_rstd = ln_stats(ft, c0, t_ftm, tt)
            for a in range(2):
                s0, s1 = a * 512, a * 512 + 512
                P.emit("vector", lambda e, ft=ft, c0=c0, s0=s0, s1=s1: e.tensor_scalar(
                    scr[:, 0:512], ft[:, s0:s1], stt[:, c0 + 2:c0 + 3], stt[:, c0 + 4:c0 + 5],
                    ALU.subtract, ALU.mult),
                    waits=[t_rstd, P.tok("vector"), tk("scr")], inc=1)
                P.emit("vector", lambda e, s0=s0, s1=s1: e.tensor_tensor(
                    scr[:, 512:1024], scr[:, 0:512], gam[:, s0:s1], ALU.mult),
                    waits=[P.tok("vector")], inc=1)
                P.emit("vector", lambda e, s0=s0, s1=s1: e.tensor_tensor(
                    scr[:, 512:1024], scr[:, 512:1024], bet[:, s0:s1], ALU.add), inc=1)
                P.emit("vector", lambda e, tt=tt, s0=s0, s1=s1: e.tensor_tensor(
                    xres[:, tt * D + s0: tt * D + s1],
                    xres[:, tt * D + s0: tt * D + s1], scr[:, 512:1024], ALU.add),
                    inc=1)
                last["scr"] = P.tok("vector")
        t_x1 = P.tok("vector")
        last["fused"] = t_x1
        dbg_tap("x1", xres, 8192, c)
        maybe_trunc("s8a")
        last["qk"] = P.tok("vector")
        for tt in range(NTT):
            xt = xres[:, tt * D:(tt + 1) * D]
            c0 = (tt % 4) * 8
            t_rstd = ln_stats(xt, c0, t_x1, tt)
            P.emit("vector", lambda e, xt=xt, tt=tt, c0=c0: e.tensor_scalar(
                vt[:, tt * D:(tt + 1) * D], xt, stt[:, c0 + 2:c0 + 3], stt[:, c0 + 4:c0 + 5],
                ALU.subtract, ALU.mult),
                waits=[t_rstd, P.tok("vector"), tk("vt")], inc=1)
            tile_transpose(tt, P.tok("vector"))
        t_ln1fm = P.tok("vector")
        t_ln1 = t_ln1fm
        last["lnfm"] = t_ln1fm
        last["vt"] = P.tok("tensor")
        last["qk"] = P.tok("vector")
        dbg_tap("ln1fm", lnfm, 8192, c)
        maybe_trunc("s9")

        for tt in range(NTT):
            pg = fb[tt % 2]
            pgn = "f%d" % (tt % 2)
            for dc in range(8):
                P.emit("tensor", lambda e, pg=pg, tt=tt, dc=dc: e.matmul(
                    pg[:, 0:E], lnfm[:, dc * 1024 + tt * 128: dc * 1024 + tt * 128 + 128],
                    wg_sb[:, dc * 8:dc * 8 + 8], start=(dc == 0), stop=False),
                    waits=[t_ln1fm, tk(pgn), t_smallcast] if dc == 0 else ())
            P.emit("tensor", lambda e, pg=pg: e.matmul(
                pg[:, 0:E], onesk1(), rowc[0:1, R_BG:R_BG + E], start=False, stop=True), inc=1)
            t_g = P.tok("tensor")
            P.emit("scalar", lambda e, pg=pg, tt=tt: e.activation(
                gst[:, tt * 8:tt * 8 + 8], pg[:, 0:E], AF.Copy),
                waits=[t_g, tk("gst")], inc=1)
            last[pgn] = P.tok("scalar")
        t_logits = P.tok("scalar")
        P.emit("scalar", lambda e: e.activation(gst[:, 64:128], gst[:, 0:64], AF.Exp),
               waits=[t_logits], inc=1)
        t_ge = P.tok("scalar")
        P.emit("vector", lambda e: e.tensor_reduce(
            gst[:, 128:136], gst[:, 64:128].rearrange("p (a b) -> p a b", b=8), AX.X, ALU.add),
            waits=[t_ge], inc=1)
        P.emit("vector", lambda e: e.reciprocal(gst[:, 136:144], gst[:, 128:136]),
               waits=[P.tok("vector")], inc=1)
        t_rc = P.tok("vector")
        for tt in range(NTT):
            P.emit("vector", lambda e, tt=tt: e.tensor_single_scalar(
                gst[:, 64 + tt * 8:64 + tt * 8 + 8], gst[:, 64 + tt * 8:64 + tt * 8 + 8],
                gst[:, 136 + tt:137 + tt], ALU.mult),
                waits=[t_rc], inc=1)
        t_probs = P.tok("vector")
        P.emit("vector", lambda e: e.tensor_reduce(
            gst[:, 144:152], gst[:, 64:128].rearrange("p (a b) -> p a b", b=8), AX.X, ALU.max),
            waits=[t_probs], inc=1)
        for tt in range(NTT):
            P.emit("vector", lambda e, tt=tt: e.tensor_single_scalar(
                gst[:, 160:168], gst[:, 64 + tt * 8:64 + tt * 8 + 8],
                gst[:, 144 + tt:145 + tt], ALU.is_equal),
                waits=[P.tok("vector")], inc=1)
            P.emit("vector", lambda e: e.tensor_tensor(
                gst[:, 160:168], gst[:, 160:168], iot[:, 8:16], ALU.mult),
                waits=[P.tok("vector")], inc=1)
            P.emit("vector", lambda e: e.tensor_single_scalar(
                gst[:, 160:168], gst[:, 160:168], 8.0, ALU.add),
                waits=[P.tok("vector")], inc=1)
            P.emit("vector", lambda e, tt=tt: e.tensor_reduce(
                gst[:, 152 + tt:153 + tt], gst[:, 160:168], AX.X, ALU.min),
                waits=[P.tok("vector")], inc=1)
            P.emit("vector", lambda e, tt=tt: e.tensor_single_scalar(
                gst[:, 160:168], iot[:, 0:8], gst[:, 152 + tt:153 + tt], ALU.is_equal),
                waits=[P.tok("vector")], inc=1)
            P.emit("vector", lambda e, tt=tt: e.tensor_tensor(
                gst[:, 192 + tt * 8:192 + tt * 8 + 8], gst[:, 160:168],
                gst[:, 64 + tt * 8:64 + tt * 8 + 8], ALU.mult),
                waits=[P.tok("vector")], inc=1)
        t_coef = P.tok("vector")
        last["gst"] = t_coef
        dbg_tap("gates", gst, 256, c)
        maybe_trunc("gate")

        t_w1pre = t_w1l
        t_w2pre = t_w2l
        for ex in range(E):
            t_w1_cur = t_w1pre
            t_w2_cur = t_w2pre
            gi = 0
            for hc in range(NHC):
                b0 = (gi % 2) * 2
                pa, pc = fb[b0], fb[b0 + 1]
                pan, pcn = "f%d" % b0, "f%d" % (b0 + 1)
                wl = ([t_w1_cur, t_ln1fm, tk(pan), tk(pcn)] if gi < 2
                      else [tk(pan), tk(pcn)])
                for dc in range(8):
                    P.emit("tensor", lambda e, pa=pa, dc=dc, hc=hc: e.matmul(
                        pa[:],
                        wbuf[:, dc * 2048 + hc * 128: dc * 2048 + hc * 128 + 128],
                        lnfm[:, dc * 1024: dc * 1024 + 512],
                        start=(dc == 0), stop=(dc == 7)),
                        waits=wl if dc == 0 else ())
                    P.emit("tensor", lambda e, pc=pc, dc=dc, hc=hc: e.matmul(
                        pc[:],
                        wbuf[:, dc * 2048 + hc * 128: dc * 2048 + hc * 128 + 128],
                        lnfm[:, dc * 1024 + 512: dc * 1024 + 1024],
                        start=(dc == 0), stop=(dc == 7)),
                        inc=(1 if dc == 7 else 0))
                t_mm = P.tok("tensor")
                P.emit("scalar", lambda e, pa=pa, hc=hc, ex=ex: e.activation(
                    qk_h[:, hc * 1024: hc * 1024 + 512], pa[:],
                    AF.Gelu, bias=b1c[:, ex * 16 + hc: ex * 16 + hc + 1]),
                    waits=[t_mm, tk("qk")], inc=1)
                last[pan] = P.tok("scalar")
                P.emit("scalar", lambda e, pc=pc, hc=hc, ex=ex: e.activation(
                    qk_h[:, hc * 1024 + 512: hc * 1024 + 1024], pc[:],
                    AF.Gelu, bias=b1c[:, ex * 16 + hc: ex * 16 + hc + 1]),
                    inc=1)
                last[pcn] = P.tok("scalar")
                gi += 1
            t_h = P.tok("scalar")
            t_hmm = P.tok("tensor")
            if ex + 1 < E:
                P.emit("sync", lambda e, ex=ex: e.dma_start(
                    out=wbuf[:, 0:16384].rearrange("p (n k) -> p n k", k=H),
                    in_=w1s[ex + 1].rearrange("(n p) k -> p n k", p=128)),
                    waits=[t_w1c[ex + 1], t_hmm], inc=16)
                t_w1pre = P.tok("sync")
            ybpairs = [(4, 5), (0, 1)]
            gi = 0
            for tt in range(NTT):
                b0, b1_ = ybpairs[gi % 2]
                pa, pc = fb[b0], fb[b1_]
                pan, pcn = "f%d" % b0, "f%d" % b1_
                wl = ([t_h, t_w2_cur, tk(pan), tk(pcn)] if gi < 2
                      else [tk(pan), tk(pcn)])
                gi += 1
                for hc in range(NHC):
                    P.emit("tensor", lambda e, pa=pa, hc=hc, tt=tt: e.matmul(
                        pa[:],
                        qk_h[:, hc * 1024 + tt * 128: hc * 1024 + tt * 128 + 128],
                        wbuf[:, 16384 + hc * 1024: 16384 + hc * 1024 + 512],
                        start=(hc == 0), stop=False),
                        waits=wl if hc == 0 else ())
                    P.emit("tensor", lambda e, pc=pc, hc=hc, tt=tt: e.matmul(
                        pc[:],
                        qk_h[:, hc * 1024 + tt * 128: hc * 1024 + tt * 128 + 128],
                        wbuf[:, 16384 + hc * 1024 + 512: 16384 + hc * 1024 + 1024],
                        start=(hc == 0), stop=False))
                P.emit("tensor", lambda e, pa=pa: e.matmul(
                    pa[:], onesk1(), rows2[0:1, 0:512],
                    start=False, stop=True))
                P.emit("tensor", lambda e, pc=pc: e.matmul(
                    pc[:], onesk1(), rows2[0:1, 512:1024],
                    start=False, stop=True), inc=1)
                t_mm = P.tok("tensor")
                for dh, pb2, pbn2 in [(0, pa, pan), (1, pc, pcn)]:
                    P.emit("vector", lambda e, pb2=pb2, tt=tt, dh=dh, ex=ex: e.scalar_tensor_tensor(
                        xres[:, tt * 1024 + dh * 512: tt * 1024 + dh * 512 + 512], pb2[:],
                        gst[:, 192 + tt * 8 + ex: 192 + tt * 8 + ex + 1],
                        xres[:, tt * 1024 + dh * 512: tt * 1024 + dh * 512 + 512],
                        ALU.mult, ALU.add),
                        waits=([t_mm, t_coef] if dh == 0 else ()), inc=1)
                    last[pbn2] = P.tok("vector")
            t_ymm = P.tok("tensor")
            last["qk"] = t_ymm
            last["rows2"] = t_ymm
            if ex + 1 < E:
                P.emit("sync", lambda e, ex=ex: e.dma_start(
                    out=wbuf[:, 16384:32768].rearrange("p (n k) -> p n k", k=D),
                    in_=w2s[ex + 1, 0:H, :].rearrange("(n p) k -> p n k", p=128)),
                    waits=[t_w2c[ex + 1], t_ymm], inc=16)
                P.emit("sync", lambda e, ex=ex: e.dma_start(
                    out=rows2[0:1, 0:D], in_=w2s[ex + 1, H:H + 1, :]),
                    waits=[t_ymm], inc=16)
                t_w2pre = P.tok("sync")
        last["wb1"] = P.tok("tensor")
        last["wb2"] = P.tok("tensor")
        last["wb3"] = P.tok("tensor")
        t_ffn = P.tok("vector")
        last["gst"] = t_ffn

        P.emit("sync", lambda e, c=c: e.dma_start(
            out=out_h[c].rearrange("(a p) d -> p a d", p=128),
            in_=xres[:].rearrange("p (a d) -> p a d", d=D)),
            waits=[t_ffn], inc=16)
        last["xres"] = P.tok("sync")

    try:
        for c in range(n_chunks):
            emit_chunk(c)
    except _Trunc:
        pass

    P.emit("sync", lambda e: e.nop(), waits=[P.tok("sync")])

    P.build()
    ctx.close()
    return nc


def _host_prep(inputs):
    asc = np.ascontiguousarray
    f32 = np.float32
    bf = ml_dtypes.bfloat16
    d = {}
    d["pt_l"] = asc(np.asarray(inputs["L0_attn_proj_local"], dtype=f32).T)
    d["pt_g"] = asc(np.asarray(inputs["L0_attn_proj_global"], dtype=f32).T)
    d["wint_l"] = asc(np.asarray(inputs["L0_attn_Wl_in"], dtype=f32).T)
    d["wint_g"] = asc(np.asarray(inputs["L0_attn_Wg_in"], dtype=f32).T)
    d["wot_l"] = asc(np.asarray(inputs["L0_attn_Wl_out"], dtype=f32).T)
    d["wot_g"] = asc(np.asarray(inputs["L0_attn_Wg_out"], dtype=f32).T)
    d["w1"] = asc(np.asarray(inputs["L0_ffn_W1"], dtype=f32))
    d["w2"] = asc(np.asarray(inputs["L0_ffn_W2"], dtype=f32))
    d["b2"] = asc(np.asarray(inputs["L0_ffn_b2"], dtype=f32))
    for name, key in [("l", "L0_attn_bl_in"), ("g", "L0_attn_bg_in")]:
        b = np.asarray(inputs[key], dtype=f32)
        d["bqk_" + name] = asc(b[0:2048].reshape(16, 128).T)
        d["bv_" + name] = asc(b[2048:3072].reshape(1, 1024))
    d["bout_l"] = asc(np.asarray(inputs["L0_attn_bl_out"], dtype=f32).reshape(1, 1024))
    d["bout_g"] = asc(np.asarray(inputs["L0_attn_bg_out"], dtype=f32).reshape(1, 1024))
    d["gam"] = asc(np.broadcast_to(np.asarray(inputs["L0_attn_ln_gamma"], dtype=f32), (128, D)))
    d["bet"] = asc(np.broadcast_to(np.asarray(inputs["L0_attn_ln_beta"], dtype=f32), (128, D)))
    d["fus"] = asc(np.broadcast_to(np.asarray(inputs["L0_attn_fusion_w"], dtype=f32), (128, 2)))
    d["wgg"] = asc(np.asarray(inputs["L0_ffn_Wg"], dtype=f32).reshape(8, 128, 8).transpose(1, 0, 2).reshape(128, 64))
    d["bgg"] = asc(np.asarray(inputs["L0_ffn_bg"], dtype=f32).reshape(1, E))
    d["b1c"] = asc(np.asarray(inputs["L0_ffn_b1"], dtype=f32).reshape(8, 16, 128).transpose(2, 0, 1).reshape(128, 128))
    d["identc"] = np.eye(128, dtype=bf)
    st = np.zeros((128, 128), dtype=bf)
    st[:, 0:64] = 1
    sbm = np.zeros((128, 128), dtype=bf)
    sbm[:, 64:128] = 1
    d["seltc"] = st
    d["selbc"] = sbm
    io = np.zeros((128, 24), dtype=f32)
    io[:, 0:8] = np.arange(8, dtype=f32)
    io[:, 8:16] = np.arange(8, dtype=f32) - 8.0
    io[:, 16] = EPS
    d["iotac"] = io
    d["onesbc"] = np.ones((1, 640), dtype=bf)
    tril = np.tril(np.ones((128, 128), dtype=np.float64), -1) * 8.0
    d["tril8bc"] = tril.astype(bf)
    u1 = np.zeros((128, 128)); u1[64:128, 0:64] = np.eye(64)
    v1 = np.zeros((128, 128)); v1[64:128, :] = tril[0:64, :]
    u2 = np.zeros((128, 128)); u2[64:128, 64:128] = np.eye(64)
    v2 = np.zeros((128, 128)); v2[64:128, :] = tril[64:128, :]
    d["mu1c"] = u1.astype(bf)
    d["mv1c"] = v1.astype(bf)
    d["mu2c"] = u2.astype(bf)
    d["mv2c"] = v2.astype(bf)
    return d


def kernel(**inputs):
    x = np.asarray(inputs["L0_attn_x"], dtype=np.float32)
    B, S, _ = x.shape
    chunks = x.reshape(B * S // T, T, D)
    n_per = chunks.shape[0] // NCORES
    weights = _host_prep(inputs)

    nc = build(n_chunks=n_per)
    in_maps = []
    for ci in range(NCORES):
        m = dict(weights)
        m["x"] = np.ascontiguousarray(chunks[ci * n_per:(ci + 1) * n_per])
        in_maps.append(m)
    res = run_bass_kernel_spmd(nc, in_maps, core_ids=list(range(NCORES)))
    out = np.concatenate([r["out"] for r in res.results], axis=0)
    return out.reshape(B, S, D)


# version 07
# speedup vs baseline: 1.1636x; 1.1636x over previous
import numpy as np
import ml_dtypes
from contextlib import ExitStack
import concourse.bass as bass
import concourse.mybir as mybir
from concourse.bass_utils import run_bass_kernel_spmd

F32 = mybir.dt.float32
BF16 = mybir.dt.bfloat16
AF = mybir.ActivationFunctionType
ALU = mybir.AluOpType
AX = mybir.AxisListType

NCORES = 8
D = 1024
T = 1024
NCHUNK = 8
NTT = 8
NPAIR = 8
NKT = 8
H, NHC = 2048, 16
E = 8
EPS = 1e-5


class Prog:
    ENGS = ["sync", "gpsimd", "tensor", "vector", "scalar"]

    def __init__(self, nc, ctx):
        self.nc = nc
        self.sems = {e: ctx.enter_context(nc.semaphore("S" + e)) for e in self.ENGS}
        self.cnt = {e: 0 for e in self.ENGS}
        self.ops = {e: [] for e in self.ENGS}

    def emit(self, e, fn, waits=(), inc=0):
        ws = tuple((we, wv) for (we, wv) in waits if we is not None and wv > 0)
        self.ops[e].append((fn, ws, inc))
        if inc:
            self.cnt[e] += inc
        return (e, self.cnt[e])

    def tok(self, e):
        return (e, self.cnt[e])

    def build(self):
        with self.nc.Block() as block:
            for e in self.ENGS:
                dec = getattr(block, e)

                def _section(eng, _e=e):
                    seen = {}
                    for (fn, ws, inc) in self.ops[_e]:
                        for (we, wv) in ws:
                            if seen.get(we, 0) >= wv:
                                continue
                            eng.wait_ge(self.sems[we], wv)
                            seen[we] = wv
                        ins = fn(eng)
                        if inc:
                            ins.then_inc(self.sems[_e], inc)
                dec(_section)


def build(n_chunks=NCHUNK, dbg=None, trunc=None):
    nc = bass.Bass("TRN2", target_bir_lowering=False)
    dram = lambda name, shape, dt, kind: nc.dram_tensor(name, shape, dt, kind=kind).ap()

    x_h = dram("x", [n_chunks, T, D], F32, "ExternalInput")
    out_h = dram("out", [n_chunks, T, D], F32, "ExternalOutput")
    pt_l = dram("pt_l", [D, D], F32, "ExternalInput")
    pt_g = dram("pt_g", [D, D], F32, "ExternalInput")
    wint_l = dram("wint_l", [D, 3 * D], F32, "ExternalInput")
    wint_g = dram("wint_g", [D, 3 * D], F32, "ExternalInput")
    wot_l = dram("wot_l", [D, D], F32, "ExternalInput")
    wot_g = dram("wot_g", [D, D], F32, "ExternalInput")
    w1_h = dram("w1", [E, D, H], F32, "ExternalInput")
    w2_h = dram("w2", [E, H, D], F32, "ExternalInput")
    b2_h = dram("b2", [E, D], F32, "ExternalInput")
    bqk_l_h = dram("bqk_l", [128, 16], F32, "ExternalInput")
    bqk_g_h = dram("bqk_g", [128, 16], F32, "ExternalInput")
    bv_l_h = dram("bv_l", [1, D], F32, "ExternalInput")
    bv_g_h = dram("bv_g", [1, D], F32, "ExternalInput")
    bout_l_h = dram("bout_l", [1, D], F32, "ExternalInput")
    bout_g_h = dram("bout_g", [1, D], F32, "ExternalInput")
    gam_h = dram("gam", [128, D], F32, "ExternalInput")
    bet_h = dram("bet", [128, D], F32, "ExternalInput")
    fus_h = dram("fus", [128, 2], F32, "ExternalInput")
    wg_h = dram("wgg", [128, 64], F32, "ExternalInput")
    bg_h = dram("bgg", [1, E], F32, "ExternalInput")
    b1c_h = dram("b1c", [128, 128], F32, "ExternalInput")
    ident_h = dram("identc", [128, 128], BF16, "ExternalInput")
    selt_h = dram("seltc", [128, 128], BF16, "ExternalInput")
    selb_h = dram("selbc", [128, 128], BF16, "ExternalInput")
    iota_h = dram("iotac", [128, 24], F32, "ExternalInput")
    onesb_h = dram("onesbc", [1, 640], BF16, "ExternalInput")
    tril8b_h = dram("tril8bc", [128, 128], BF16, "ExternalInput")
    mu1_h = dram("mu1c", [128, 128], BF16, "ExternalInput")
    mv1_h = dram("mv1c", [128, 128], BF16, "ExternalInput")
    mu2_h = dram("mu2c", [128, 128], BF16, "ExternalInput")
    mv2_h = dram("mv2c", [128, 128], BF16, "ExternalInput")

    dbg_h = dram("dbgout", [128, 16384], F32, "ExternalOutput") if dbg else None
    wints_l = dram("wints_l", [D, 3 * D], BF16, "Internal")
    wints_g = dram("wints_g", [D, 3 * D], BF16, "Internal")
    m_l = dram("m_l", [D, 3 * D], BF16, "Internal")
    m_g = dram("m_g", [D, 3 * D], BF16, "Internal")
    wos_l = dram("wos_l", [D, D], BF16, "Internal")
    wos_g = dram("wos_g", [D, D], BF16, "Internal")
    w1s = dram("w1s", [E, D, H], BF16, "Internal")
    w2s = dram("w2s", [E, H + 1, D], BF16, "Internal")

    ctx = ExitStack()
    sb = lambda name, shape, dt: ctx.enter_context(nc.sbuf_tensor(name, shape, dt))
    psum = lambda name, shape, dt: ctx.enter_context(nc.psum_tensor(name, shape, dt))

    xres = sb("xres", [128, NTT * D], F32)
    lnfm = sb("lnfm", [128, 8 * T], BF16)
    qk_h = sb("qk_h", [128, 16384], BF16)
    vt = sb("vt", [128, NTT * D], BF16)
    fused = sb("fusedb", [128, 8 * T], BF16)
    expt = sb("expt", [128, 2048], BF16)
    wbuf = sb("wbuf", [128, 32768], BF16)
    scr = sb("scr", [128, 1024], F32)
    gst = sb("gst", [128, 256], F32)
    stt = sb("stt", [128, 64], F32)
    gam = sb("gamS", [128, D], F32)
    bet = sb("betS", [128, D], F32)
    fusw = sb("fuswS", [128, 4], F32)
    bqkl = sb("bqklS", [128, 16], F32)
    bqkg = sb("bqkgS", [128, 16], F32)
    b1c = sb("b1cS", [128, 128], F32)
    wg_sb = sb("wgS", [128, 64], BF16)
    iot = sb("iotS", [128, 24], F32)
    ident = sb("identS", [128, 128], BF16)
    selt = sb("seltS", [128, 128], BF16)
    selb = sb("selbS", [128, 128], BF16)
    tril8b = sb("tril8bS", [128, 128], BF16)
    mu1 = sb("mu1S", [128, 128], BF16)
    mv1 = sb("mv1S", [128, 128], BF16)
    mu2 = sb("mu2S", [128, 128], BF16)
    mv2 = sb("mv2S", [128, 128], BF16)
    rowc = sb("rowcS", [128, 4864], BF16)
    rows2 = sb("rows2S", [128, 1024], BF16)
    R_BVL, R_BVG, R_BOL, R_BOG, R_BG, R_ON512, R_ONK1 = 0, 1024, 2048, 3072, 4096, 4104, 4616

    fb = [psum("f%d" % i, [128, 512], F32) for i in range(6)]
    ptrs = [psum("ptrb%d" % i, [128, 1024], BF16) for i in range(2)]

    P = Prog(nc, ctx)
    last = {}

    def tk(name):
        return last.get(name, (None, 0))

    def ones512():
        return rowc[0:1, R_ON512:R_ON512 + 512]

    def onesk1():
        return rowc[0:1, R_ONK1:R_ONK1 + 128]

    def dbg_tap(name, buf, cols, chunk):
        if dbg is None or dbg != name or chunk != 0:
            return
        wl = [P.tok(e) for e in Prog.ENGS]
        t_tap = P.emit("gpsimd", lambda e: e.dma_start(out=dbg_h[:, 0:cols], in_=buf[:, 0:cols]),
                       waits=wl, inc=16)
        for e2 in Prog.ENGS:
            P.emit(e2, lambda e: e.nop(), waits=[t_tap])

    for src, dst in [(gam_h, gam), (bet_h, bet), (bqk_l_h, bqkl), (bqk_g_h, bqkg),
                     (b1c_h, b1c), (iota_h, iot), (ident_h, ident),
                     (selt_h, selt), (selb_h, selb), (tril8b_h, tril8b),
                     (mu1_h, mu1), (mv1_h, mv1), (mu2_h, mu2), (mv2_h, mv2)]:
        P.emit("sync", lambda e, s=src, d=dst: e.dma_start(out=d[:], in_=s), inc=16)
    P.emit("sync", lambda e: e.dma_start(out=fusw[:, 0:2], in_=fus_h), inc=16)
    P.emit("sync", lambda e: e.dma_start(out=rowc[0:1, R_ON512:R_ONK1 + 128], in_=onesb_h), inc=16)
    t_const = P.tok("sync")

    P.emit("gpsimd", lambda e: e.dma_start(out=wg_sb[:], in_=wg_h), inc=16)
    for src, dst in [(bv_l_h, rowc[0:1, R_BVL:R_BVL + D]), (bv_g_h, rowc[0:1, R_BVG:R_BVG + D]),
                     (bout_l_h, rowc[0:1, R_BOL:R_BOL + D]), (bout_g_h, rowc[0:1, R_BOG:R_BOG + D]),
                     (bg_h, rowc[0:1, R_BG:R_BG + E])]:
        P.emit("gpsimd", lambda e, s=src, d=dst: e.dma_start(out=d, in_=s), inc=16)
    t_smallcast = P.tok("gpsimd")

    P.emit("gpsimd", lambda e: e.dma_start(out=wints_l, in_=wint_l), inc=16)
    t_wints = {0: P.tok("gpsimd")}
    P.emit("gpsimd", lambda e: e.dma_start(out=wints_g, in_=wint_g), inc=16)
    t_wints[1] = P.tok("gpsimd")
    P.emit("gpsimd", lambda e: e.dma_start(out=wos_l, in_=wot_l), inc=16)
    P.emit("gpsimd", lambda e: e.dma_start(out=wos_g, in_=wot_g), inc=16)
    t_wos = P.tok("gpsimd")
    t_w1c = [None] * E
    t_w2c = [None] * E
    for e_ in range(E):
        P.emit("gpsimd", lambda e, i=e_: e.dma_start(out=w1s[i], in_=w1_h[i]), inc=16)
        t_w1c[e_] = P.tok("gpsimd")
        P.emit("gpsimd", lambda e, i=e_: e.dma_start(out=w2s[i, 0:H, :], in_=w2_h[i]), inc=16)
        P.emit("gpsimd", lambda e, i=e_: e.dma_start(
            out=w2s[i, H:H + 1, :], in_=b2_h[i:i + 1, :]), inc=16)
        t_w2c[e_] = P.tok("gpsimd")

    P.emit("scalar", lambda e: e.activation(stt[:, 0:2], fusw[:, 0:2], AF.Exp),
           waits=[t_const], inc=1)
    t_fx = P.tok("scalar")
    P.emit("vector", lambda e: e.tensor_reduce(stt[:, 2:3], stt[:, 0:2], AX.X, ALU.add),
           waits=[t_fx], inc=1)
    P.emit("vector", lambda e: e.reciprocal(stt[:, 3:4], stt[:, 2:3]),
           waits=[P.tok("vector")], inc=1)
    P.emit("vector", lambda e: e.tensor_single_scalar(fusw[:, 2:4], stt[:, 0:2], stt[:, 3:4], ALU.mult),
           waits=[P.tok("vector")], inc=1)
    t_fusw = P.tok("vector")
    w0col = fusw[:, 2:3]
    w1col = fusw[:, 3:4]

    t_mready = {}
    for which, pt_src, wints_s, m_dst in [(0, pt_l, wints_l, m_l), (1, pt_g, wints_g, m_g)]:
        P.emit("gpsimd", lambda e, s=pt_src: e.dma_start(
            out=wbuf[:, 0:8192].rearrange("p (n k) -> p n k", k=1024),
            in_=s.rearrange("(n p) k -> p n k", p=128)),
            waits=[tk("wb1")], inc=16)
        t_pt = P.tok("gpsimd")
        P.emit("sync", lambda e, s=wints_s: e.dma_start(
            out=wbuf[:, 8192:32768].rearrange("p (n k) -> p n k", k=3072),
            in_=s.rearrange("(n p) k -> p n k", p=128)),
            waits=[t_wints[which], tk("wb2"), tk("wb3")], inc=16)
        t_wi = P.tok("sync")
        for ac in range(8):
            slot = ac % 2
            for fh in range(6):
                pb = fb[fh % 4]
                pbn = "f%d" % (fh % 4)
                for bc in range(8):
                    P.emit("tensor", lambda e, pb=pb, bc=bc, ac=ac, fh=fh: e.matmul(
                        pb[:], wbuf[:, bc * 1024 + ac * 128: bc * 1024 + ac * 128 + 128],
                        wbuf[:, 8192 + bc * 3072 + fh * 512: 8192 + bc * 3072 + fh * 512 + 512],
                        start=(bc == 0), stop=(bc == 7)),
                        waits=[t_pt, t_wi, tk(pbn)] if bc == 0 else (),
                        inc=(1 if bc == 7 else 0))
                t_mm = P.tok("tensor")
                P.emit("scalar", lambda e, pb=pb, slot=slot, fh=fh: e.activation(
                    vt[:, slot * 3072 + fh * 512: slot * 3072 + fh * 512 + 512], pb[:], AF.Copy),
                    waits=[t_mm, tk("vts%d" % slot)], inc=1)
                last[pbn] = P.tok("scalar")
            t_ev = P.tok("scalar")
            P.emit("sync", lambda e, ac=ac, slot=slot, d=m_dst: e.dma_start(
                out=d[ac * 128:(ac + 1) * 128, :], in_=vt[:, slot * 3072:(slot + 1) * 3072]),
                waits=[t_ev], inc=16)
            last["vts%d" % slot] = P.tok("sync")
        last["wb1"] = P.tok("tensor")
        last["wb2"] = P.tok("tensor")
        last["wb3"] = P.tok("tensor")
        last["vt"] = P.tok("sync")
        t_mready[which] = P.tok("sync")


    def ln_stats(src_tile, c0, src_tok, sqslot):
        sq = qk_h[:, sqslot * 1024:(sqslot + 1) * 1024]
        P.emit("scalar", lambda e: e.activation(sq, src_tile, AF.Square),
               waits=[src_tok, tk("qk"), tk("lnsq%d" % sqslot)], inc=1)
        t_sq = P.tok("scalar")
        P.emit("vector", lambda e: e.tensor_reduce(stt[:, c0:c0 + 1], src_tile, AX.X, ALU.add),
               waits=[src_tok, tk("sttc%d" % c0)], inc=1)
        P.emit("vector", lambda e: e.tensor_reduce(stt[:, c0 + 1:c0 + 2], sq, AX.X, ALU.add),
               waits=[t_sq], inc=1)
        last["lnsq%d" % sqslot] = P.tok("vector")
        P.emit("vector", lambda e: e.tensor_single_scalar(
            stt[:, c0 + 2:c0 + 3], stt[:, c0:c0 + 1], 1.0 / D, ALU.mult),
            waits=[P.tok("vector")], inc=1)
        P.emit("vector", lambda e: e.tensor_tensor(
            stt[:, c0 + 5:c0 + 6], stt[:, c0 + 2:c0 + 3], stt[:, c0 + 2:c0 + 3], ALU.mult),
            waits=[P.tok("vector")], inc=1)
        P.emit("vector", lambda e: e.tensor_single_scalar(
            stt[:, c0 + 6:c0 + 7], stt[:, c0 + 1:c0 + 2], 1.0 / D, ALU.mult),
            waits=[P.tok("vector")], inc=1)
        P.emit("vector", lambda e: e.tensor_tensor(
            stt[:, c0 + 3:c0 + 4], stt[:, c0 + 6:c0 + 7], stt[:, c0 + 5:c0 + 6], ALU.subtract),
            waits=[P.tok("vector")], inc=1)
        t_var = P.tok("vector")
        P.emit("scalar", lambda e: e.activation(stt[:, c0 + 5:c0 + 6], stt[:, c0 + 3:c0 + 4],
                                                AF.Ln, bias=iot[:, 16:17]), waits=[t_var], inc=1)
        P.emit("scalar", lambda e: e.activation(stt[:, c0 + 4:c0 + 5], stt[:, c0 + 5:c0 + 6],
                                                AF.Exp, scale=-0.5),
               waits=[P.tok("scalar")], inc=1)
        last["sttc%d" % c0] = P.tok("scalar")
        return P.tok("scalar")

    def tile_transpose(tt, t_ts):
        pb = ptrs[tt % 2]
        pn = "ptr%d" % (tt % 2)
        t_prev_ev = tk(pn)
        for j in range(8):
            P.emit("tensor", lambda e, i=tt, j=j, pb=pb: e.transpose(
                pb[:, j * 128:(j + 1) * 128],
                vt[:, i * 1024 + j * 128: i * 1024 + j * 128 + 128], ident[:]),
                waits=[t_ts, t_prev_ev] if j == 0 else (), inc=(1 if j == 7 else 0))
        t_tr = P.tok("tensor")
        dst = lnfm[:].rearrange("p (j k) -> p j k", k=1024)[:, :, tt * 128:(tt + 1) * 128]
        srcv = pb[:].rearrange("p (j k) -> p j k", k=128)
        P.emit("vector", lambda e, dst=dst, srcv=srcv: e.tensor_copy(dst, srcv),
               waits=[t_tr, tk("lnfm")], inc=1)
        last[pn] = P.tok("vector")

    def emit_sums_av(i, h01, kt, t_e, sm, smn, ub, ubn, sel, hh, ec):
        P.emit("tensor", lambda e, ub=ub, hh=hh, kt=kt, ec=ec, h01=h01: e.matmul(
            ub[h01 * 64:h01 * 64 + 64, :],
            vt[:, kt * 1024 + hh * 64: kt * 1024 + hh * 64 + 64],
            expt[:, ec:ec + 512],
            start=(kt == 0), stop=(kt == NKT - 1)),
            waits=([t_e, tk("vready"), tk(ubn)] if (h01 == 0 and kt == 0) else [t_e]),
            inc=0)
        P.emit("tensor", lambda e, sm=sm, sel=sel, ec=ec, h01=h01, kt=kt: e.matmul(
            sm[:], sel[:], expt[:, ec:ec + 512],
            start=(h01 == 0 and kt == 0), stop=(h01 == 1 and kt == NKT - 1)),
            waits=([tk(smn)] if (h01 == 0 and kt == 0) else ()),
            inc=(1 if (h01 == 1 and kt == NKT - 1) else 0))

    class _Trunc(Exception):
        pass

    def maybe_trunc(point):
        if trunc is not None and trunc == point:
            t_all = [P.tok(e) for e in Prog.ENGS]
            P.emit("sync", lambda e: e.dma_start(
                out=out_h[0].rearrange("(a p) d -> p a d", p=128),
                in_=xres[:].rearrange("p (a d) -> p a d", d=D)),
                waits=t_all, inc=16)
            raise _Trunc()

    def emit_chunk(c):
        P.emit("sync", lambda e: e.dma_start(
            out=xres[:].rearrange("p (a d) -> p a d", d=D),
            in_=x_h[c].rearrange("(a p) d -> p a d", p=128)),
            waits=[tk("xres")], inc=16)
        t_x = P.tok("sync")

        for tt in range(NTT):
            xt = xres[:, tt * D:(tt + 1) * D]
            c0 = (tt % 4) * 8
            t_rstd = ln_stats(xt, c0, t_x, tt)
            P.emit("vector", lambda e, xt=xt, tt=tt, c0=c0: e.tensor_scalar(
                vt[:, tt * D:(tt + 1) * D], xt, stt[:, c0 + 2:c0 + 3], stt[:, c0 + 4:c0 + 5],
                ALU.subtract, ALU.mult),
                waits=[t_rstd, P.tok("vector"), tk("vt")], inc=1)
            tile_transpose(tt, P.tok("vector"))
        t_lnfm = P.tok("vector")
        last["lnfm"] = t_lnfm
        last["vt"] = P.tok("tensor")
        last["qk"] = P.tok("vector")
        dbg_tap("ln0fm", lnfm, 8192, c)
        maybe_trunc("s2")

        def load_m(which):
            m_src = m_l if which == 0 else m_g
            P.emit("sync", lambda e, s=m_src: e.dma_start(
                out=wbuf[:, 0:24576].rearrange("p (n k) -> p n k", k=3072),
                in_=s.rearrange("(n p) k -> p n k", p=128)),
                waits=[tk("wb1"), tk("wb2"), t_mready[which]], inc=16)
            return P.tok("sync")

        def load_wo(which):
            wo_src = wos_l if which == 0 else wos_g
            P.emit("sync", lambda e, s=wo_src: e.dma_start(
                out=wbuf[:, 24576:32768].rearrange("p (n k) -> p n k", k=1024),
                in_=s.rearrange("(n p) k -> p n k", p=128)),
                waits=[t_wos, tk("wb3")], inc=16)
            return P.tok("sync")

        def qkv(which, t_wload):
            bqk = bqkl if which == 0 else bqkg
            rbv = R_BVL if which == 0 else R_BVG
            gi = 0
            for part in range(2):
                for fc in range(8):
                    b0 = (gi % 2) * 2
                    pa, pc = fb[b0], fb[b0 + 1]
                    pan, pcn = "f%d" % b0, "f%d" % (b0 + 1)
                    wl = ([t_wload, t_lnfm, tk(pan), tk(pcn)] if gi < 2
                          else [tk(pan), tk(pcn)])
                    for dc in range(8):
                        P.emit("tensor", lambda e, pa=pa, dc=dc, fc=fc, part=part: e.matmul(
                            pa[:],
                            wbuf[:, dc * 3072 + part * 1024 + fc * 128: dc * 3072 + part * 1024 + fc * 128 + 128],
                            lnfm[:, dc * 1024: dc * 1024 + 512],
                            start=(dc == 0), stop=(dc == 7)),
                            waits=wl if dc == 0 else ())
                        P.emit("tensor", lambda e, pc=pc, dc=dc, fc=fc, part=part: e.matmul(
                            pc[:],
                            wbuf[:, dc * 3072 + part * 1024 + fc * 128: dc * 3072 + part * 1024 + fc * 128 + 128],
                            lnfm[:, dc * 1024 + 512: dc * 1024 + 1024],
                            start=(dc == 0), stop=(dc == 7)),
                            inc=(1 if dc == 7 else 0))
                    t_mm = P.tok("tensor")
                    dst_c = part * 8192 + fc * 1024
                    P.emit("scalar", lambda e, pa=pa, dst_c=dst_c, bqk=bqk, part=part, fc=fc: e.activation(
                        qk_h[:, dst_c:dst_c + 512], pa[:], AF.Identity,
                        bias=bqk[:, part * 8 + fc:part * 8 + fc + 1]),
                        waits=[t_mm, tk("qk")], inc=1)
                    last[pan] = P.tok("scalar")
                    P.emit("scalar", lambda e, pc=pc, dst_c=dst_c, bqk=bqk, part=part, fc=fc: e.activation(
                        qk_h[:, dst_c + 512:dst_c + 1024], pc[:], AF.Identity,
                        bias=bqk[:, part * 8 + fc:part * 8 + fc + 1]),
                        inc=1)
                    last[pcn] = P.tok("scalar")
                    gi += 1
            for tt in range(8):
                pa, pc = fb[4], fb[5]
                wl = [t_wload, t_lnfm, tk("f4"), tk("f5"), t_smallcast]
                for dc in range(8):
                    P.emit("tensor", lambda e, pa=pa, dc=dc, tt=tt: e.matmul(
                        pa[:],
                        lnfm[:, dc * 1024 + tt * 128: dc * 1024 + tt * 128 + 128],
                        wbuf[:, dc * 3072 + 2048: dc * 3072 + 2048 + 512],
                        start=(dc == 0), stop=False),
                        waits=wl if dc == 0 else ())
                    P.emit("tensor", lambda e, pc=pc, dc=dc, tt=tt: e.matmul(
                        pc[:],
                        lnfm[:, dc * 1024 + tt * 128: dc * 1024 + tt * 128 + 128],
                        wbuf[:, dc * 3072 + 2560: dc * 3072 + 3072],
                        start=(dc == 0), stop=False))
                P.emit("tensor", lambda e, pa=pa, rbv=rbv: e.matmul(
                    pa[:], onesk1(), rowc[0:1, rbv: rbv + 512],
                    start=False, stop=True))
                P.emit("tensor", lambda e, pc=pc, rbv=rbv: e.matmul(
                    pc[:], onesk1(), rowc[0:1, rbv + 512: rbv + 1024],
                    start=False, stop=True), inc=1)
                t_mm = P.tok("tensor")
                P.emit("vector", lambda e, pa=pa, tt=tt: e.tensor_copy(
                    vt[:, tt * 1024: tt * 1024 + 512], pa[:]),
                    waits=[t_mm, tk("vt")], inc=1)
                last["f4"] = P.tok("vector")
                P.emit("vector", lambda e, pc=pc, tt=tt: e.tensor_copy(
                    vt[:, tt * 1024 + 512: tt * 1024 + 1024], pc[:]),
                    inc=1)
                last["f5"] = P.tok("vector")
            last["qkready"] = P.tok("scalar")
            last["vready"] = P.tok("vector")
            last["wb1"] = P.tok("tensor")
            last["wb2"] = P.tok("tensor")

        def attn_out_proj(which):
            rbo = R_BOL if which == 0 else R_BOG
            gi = 0
            for dc in range(8):
                b0 = (gi % 2) * 2
                pa, pc = fb[b0], fb[b0 + 1]
                pan, pcn = "f%d" % b0, "f%d" % (b0 + 1)
                wl = ([tk("oready"), tk(pan), tk(pcn)] if gi < 2 else [tk(pan), tk(pcn)])
                gi += 1
                for fc in range(8):
                    P.emit("tensor", lambda e, pa=pa, fc=fc, dc=dc: e.matmul(
                        pa[:],
                        wbuf[:, 24576 + fc * 1024 + dc * 128: 24576 + fc * 1024 + dc * 128 + 128],
                        qk_h[:, fc * 1024: fc * 1024 + 512],
                        start=(fc == 0), stop=False),
                        waits=wl if fc == 0 else ())
                    P.emit("tensor", lambda e, pc=pc, fc=fc, dc=dc: e.matmul(
                        pc[:],
                        wbuf[:, 24576 + fc * 1024 + dc * 128: 24576 + fc * 1024 + dc * 128 + 128],
                        qk_h[:, fc * 1024 + 512: fc * 1024 + 1024],
                        start=(fc == 0), stop=False))
                P.emit("tensor", lambda e, pa=pa, dc=dc, rbo=rbo: e.matmul(
                    pa[:], rowc[0:1, rbo + dc * 128: rbo + dc * 128 + 128], ones512(),
                    start=False, stop=True))
                P.emit("tensor", lambda e, pc=pc, dc=dc, rbo=rbo: e.matmul(
                    pc[:], rowc[0:1, rbo + dc * 128: rbo + dc * 128 + 128], ones512(),
                    start=False, stop=True), inc=1)
                t_mm = P.tok("tensor")
                for th, pb2, pbn2 in [(0, pa, pan), (1, pc, pcn)]:
                    dst = fused[:, dc * 1024 + th * 512: dc * 1024 + th * 512 + 512]
                    if which == 0:
                        P.emit("vector", lambda e, pb2=pb2, dst=dst: e.tensor_single_scalar(
                            dst, pb2[:], w0col, ALU.mult),
                            waits=([t_mm, t_fusw, tk("fused")] if th == 0 else ()), inc=1)
                    else:
                        P.emit("vector", lambda e, pb2=pb2, dst=dst: e.scalar_tensor_tensor(
                            dst, pb2[:], w1col, dst, ALU.mult, ALU.add),
                            waits=([t_mm, t_fusw, tk("fusedacc")] if th == 0 else ()), inc=1)
                    last[pbn2] = P.tok("vector")
            last["fusedacc"] = P.tok("vector")
            last["wb3"] = P.tok("tensor")

        t_wl_m = load_m(0)
        load_wo(0)
        qkv(0, t_wl_m)
        dbg_tap("qk_l", qk_h, 16384, c)
        dbg_tap("vt_l", vt, 8192, c)
        t_wl_mg = load_m(1)
        maybe_trunc("qkv0")

        units = [(w, pr) for w in range(8) for pr in range(NPAIR)]

        def l_scores(i):
            w, pr = units[i]
            ec = (i % 4) * 512
            P.emit("tensor", lambda e, pr=pr, w=w: e.matmul(
                fb[0][:, 0:128],
                qk_h[0:64, 8192 + pr * 1024 + w * 128: 8192 + pr * 1024 + w * 128 + 128],
                qk_h[0:64, pr * 1024 + w * 128: pr * 1024 + w * 128 + 128],
                start=True, stop=False),
                waits=[tk("qkready"), tk("f0")])
            P.emit("tensor", lambda e: e.matmul(
                fb[0][:, 0:128], ident[:], tril8b[:], start=False, stop=True), inc=1)
            t_s0 = P.tok("tensor")
            P.emit("tensor", lambda e, pr=pr, w=w: e.matmul(
                fb[1][:, 0:128],
                qk_h[64:128, 8192 + pr * 1024 + w * 128: 8192 + pr * 1024 + w * 128 + 128],
                qk_h[64:128, pr * 1024 + w * 128: pr * 1024 + w * 128 + 128],
                start=True, stop=False),
                waits=[tk("f1")])
            P.emit("tensor", lambda e: e.matmul(
                fb[1][:, 0:128], mu1[64:128, :], mv1[64:128, :], start=False, stop=False))
            P.emit("tensor", lambda e: e.matmul(
                fb[1][:, 0:128], mu2[64:128, :], mv2[64:128, :], start=False, stop=True), inc=1)
            t_s1 = P.tok("tensor")
            P.emit("scalar", lambda e, ec=ec: e.activation(
                expt[:, ec:ec + 128], fb[0][:, 0:128], AF.Exp, scale=0.125),
                waits=[t_s0], inc=1)
            last["f0"] = P.tok("scalar")
            P.emit("scalar", lambda e, ec=ec: e.activation(
                expt[:, ec + 128:ec + 256], fb[1][:, 0:128], AF.Exp, scale=0.125),
                waits=[t_s1], inc=1)
            last["f1"] = P.tok("scalar")
            return P.tok("scalar")

        def l_tail(i, t_e):
            w, pr = units[i]
            sm = fb[2 + i % 2]
            smn = "f%d" % (2 + i % 2)
            ub = fb[4 + i % 2]
            ubn = "f%d" % (4 + i % 2)
            ec = (i % 4) * 512
            P.emit("tensor", lambda e, sm=sm, ec=ec: e.matmul(
                sm[:, 0:128], selt[:], expt[:, ec:ec + 128],
                start=True, stop=False), waits=[t_e, tk(smn)])
            P.emit("tensor", lambda e, sm=sm, ec=ec: e.matmul(
                sm[:, 0:128], selb[:], expt[:, ec + 128:ec + 256],
                start=False, stop=True), inc=1)
            t_sum = P.tok("tensor")
            P.emit("tensor", lambda e, ub=ub, pr=pr, w=w, ec=ec: e.matmul(
                ub[0:64, 0:128],
                vt[:, w * 1024 + (pr * 2) * 64: w * 1024 + (pr * 2) * 64 + 64],
                expt[:, ec:ec + 128],
                start=True, stop=True), waits=[tk("vready"), tk(ubn)])
            P.emit("tensor", lambda e, ub=ub, pr=pr, w=w, ec=ec: e.matmul(
                ub[64:128, 0:128],
                vt[:, w * 1024 + (pr * 2 + 1) * 64: w * 1024 + (pr * 2 + 1) * 64 + 64],
                expt[:, ec + 128:ec + 256],
                start=True, stop=True), inc=1)
            t_av = P.tok("tensor")
            rc = (i % 2) * 512
            P.emit("vector", lambda e, sm=sm, rc=rc: e.reciprocal(
                scr[:, rc:rc + 128], sm[:, 0:128]),
                waits=[t_sum, tk("scrl%d" % (i % 2))], inc=1)
            t_r = P.tok("vector")
            last[smn] = t_r
            oc = pr * 1024 + w * 128
            P.emit("vector", lambda e, ub=ub, oc=oc, rc=rc: e.tensor_tensor(
                qk_h[0:64, oc:oc + 128], ub[0:64, 0:128], scr[0:64, rc:rc + 128], ALU.mult),
                waits=[t_av, t_r], inc=1)
            P.emit("vector", lambda e, ub=ub, oc=oc, rc=rc: e.tensor_tensor(
                qk_h[64:128, oc:oc + 128], ub[64:128, 0:128], scr[64:128, rc:rc + 128], ALU.mult),
                inc=1)
            last[ubn] = P.tok("vector")
            last["scrl%d" % (i % 2)] = P.tok("vector")

        t_es = {}
        for i in range(len(units)):
            t_es[i] = l_scores(i)
            maybe_trunc("lsc%d" % i)
            if i >= 1:
                l_tail(i - 1, t_es[i - 1])
                maybe_trunc("lu%d" % (i - 1))
        l_tail(len(units) - 1, t_es[len(units) - 1])
        last["oready"] = P.tok("vector")
        dbg_tap("oloc", qk_h, 8192, c)
        maybe_trunc("local")
        attn_out_proj(0)
        last["qk"] = P.tok("vector")
        last["vt"] = P.tok("tensor")
        dbg_tap("fusedl", fused, 8192, c)
        maybe_trunc("outproj0")

        load_wo(1)
        qkv(1, t_wl_mg)
        dbg_tap("qk_g", qk_h, 16384, c)
        dbg_tap("vt_g", vt, 8192, c)
        maybe_trunc("qkv1")
        P.emit("sync", lambda e: e.dma_start(
            out=wbuf[:, 0:16384].rearrange("p (n k) -> p n k", k=H),
            in_=w1s[0].rearrange("(n p) k -> p n k", p=128)),
            waits=[t_w1c[0], tk("wb1"), tk("wb2")], inc=16)
        t_w1l = P.tok("sync")

        gunits = [(pr, qb) for pr in range(NPAIR) for qb in range(2)]
        for i, (pr, qb) in enumerate(gunits):
            sm = fb[2 + i % 2]
            smn = "f%d" % (2 + i % 2)
            ub = fb[4 + i % 2]
            ubn = "f%d" % (4 + i % 2)
            pend = None
            for h01 in range(2):
                hh = pr * 2 + h01
                base = 64 * h01
                sel = selt if h01 == 0 else selb
                for kt in range(NKT):
                    sc = fb[kt % 2]
                    scn = "f%d" % (kt % 2)
                    P.emit("tensor", lambda e, sc=sc, base=base, kt=kt, qb=qb, pr=pr: e.matmul(
                        sc[:],
                        qk_h[base:base + 64, 8192 + pr * 1024 + kt * 128: 8192 + pr * 1024 + kt * 128 + 128],
                        qk_h[base:base + 64, pr * 1024 + qb * 512: pr * 1024 + qb * 512 + 512],
                        start=True, stop=True),
                        waits=[tk("qkready"), tk(scn)], inc=1)
                    t_s = P.tok("tensor")
                    dstart = kt * 128 - qb * 512
                    t_m = t_s
                    if 0 <= dstart < 512:
                        P.emit("vector", lambda e, sc=sc, dstart=dstart: e.tensor_tensor(
                            sc[:, dstart:dstart + 128], sc[:, dstart:dstart + 128],
                            tril8b[:], ALU.add), waits=[t_s], inc=1)
                        t_m = P.tok("vector")
                    ec = (h01 * 2 + kt % 2) * 512
                    wl = [t_m]
                    if dstart >= 512:
                        P.emit("scalar", lambda e, sc=sc, ec=ec: e.activation(
                            expt[:, ec:ec + 512], sc[:], AF.Exp, scale=0.125, bias=1.0),
                            waits=wl, inc=1)
                    elif dstart <= 0:
                        P.emit("scalar", lambda e, sc=sc, ec=ec: e.activation(
                            expt[:, ec:ec + 512], sc[:], AF.Exp, scale=0.125),
                            waits=wl, inc=1)
                    else:
                        P.emit("scalar", lambda e, sc=sc, ec=ec, ds=dstart: e.activation(
                            expt[:, ec:ec + ds], sc[:, 0:ds], AF.Exp, scale=0.125, bias=1.0),
                            waits=wl, inc=1)
                        P.emit("scalar", lambda e, sc=sc, ec=ec, ds=dstart: e.activation(
                            expt[:, ec + ds:ec + 512], sc[:, ds:512], AF.Exp, scale=0.125),
                            inc=1)
                    t_e = P.tok("scalar")
                    last[scn] = t_e
                    if pend is not None:
                        emit_sums_av(*pend)
                    pend = (i, h01, kt, t_e, sm, smn, ub, ubn, sel, hh, ec)
            emit_sums_av(*pend)
            t_gsum = P.tok("tensor")
            rc = (i % 2) * 512
            P.emit("vector", lambda e, sm=sm, rc=rc: e.reciprocal(scr[:, rc:rc + 512], sm[:]),
                   waits=[t_gsum, tk("scrg%d" % (i % 2))], inc=1)
            t_r = P.tok("vector")
            last[smn] = t_r
            oc = pr * 1024 + qb * 512
            P.emit("vector", lambda e, ub=ub, oc=oc, rc=rc: e.tensor_tensor(
                qk_h[0:64, oc:oc + 512], ub[0:64, :], scr[0:64, rc:rc + 512], ALU.mult),
                waits=[t_r], inc=1)
            P.emit("vector", lambda e, ub=ub, oc=oc, rc=rc: e.tensor_tensor(
                qk_h[64:128, oc:oc + 512], ub[64:128, :], scr[64:128, rc:rc + 512], ALU.mult),
                inc=1)
            last[ubn] = P.tok("vector")
            last["scrg%d" % (i % 2)] = P.tok("vector")
        last["oready"] = P.tok("vector")
        dbg_tap("oglo", qk_h, 8192, c)
        maybe_trunc("global")
        attn_out_proj(1)
        last["qk"] = P.tok("vector")
        last["vt"] = P.tok("tensor")
        dbg_tap("fusedfm", fused, 8192, c)
        maybe_trunc("outproj1")

        P.emit("sync", lambda e: e.dma_start(
            out=wbuf[:, 16384:32768].rearrange("p (n k) -> p n k", k=D),
            in_=w2s[0, 0:H, :].rearrange("(n p) k -> p n k", p=128)),
            waits=[t_w2c[0], tk("wb2"), tk("wb3")], inc=16)
        P.emit("sync", lambda e: e.dma_start(
            out=rows2[0:1, 0:D], in_=w2s[0, H:H + 1, :]),
            waits=[tk("rows2")], inc=16)
        t_w2l = P.tok("sync")

        t_fa = tk("fusedacc")
        t_tile = {}
        entries = [(i, j) for i in range(8) for j in range(i, 8)]
        groups, cur, nb = [], [], 0
        for en in entries:
            b = 1 if en[0] == en[1] else 2
            if nb + b > 8:
                groups.append(cur)
                cur, nb = [], 0
            cur.append(en)
            nb += b
        groups.append(cur)
        for gi2, grp in enumerate(groups):
            pb = ptrs[gi2 % 2]
            pn = "ptr%d" % (gi2 % 2)
            t_prev = tk(pn)
            blocks = []
            for (i, j) in grp:
                blocks.append((i, j))
                if i != j:
                    blocks.append((j, i))
            for bi, (a, b) in enumerate(blocks):
                P.emit("tensor", lambda e, a=a, b=b, s=bi, pb=pb: e.transpose(
                    pb[:, s * 128:(s + 1) * 128],
                    fused[:, a * 1024 + b * 128: a * 1024 + b * 128 + 128], ident[:]),
                    waits=[t_fa, t_prev] if bi == 0 else (),
                    inc=(1 if bi == len(blocks) - 1 else 0))
            t_t = P.tok("tensor")
            for bi, (a, b) in enumerate(blocks):
                P.emit("vector", lambda e, a=a, b=b, s=bi, pb=pb: e.tensor_copy(
                    fused[:, b * 1024 + a * 128: b * 1024 + a * 128 + 128], pb[:, s * 128:(s + 1) * 128]),
                    waits=[t_t] if bi == 0 else (), inc=(1 if bi == len(blocks) - 1 else 0))
            last[pn] = P.tok("vector")
            for (i, j) in grp:
                if j == 7:
                    t_tile[i] = P.tok("vector")
        t_ftm = P.tok("vector")
        dbg_tap("fusedg", fused, 8192, c)
        maybe_trunc("s7")

        t_gbs = {}

        def s8_ln2(tt):
            ft = fused[:, tt * D:(tt + 1) * D]
            vtt = vt[:, tt * D:(tt + 1) * D]
            c0 = (tt % 2) * 8
            t_rstd2 = ln_stats(ft, c0, t_tile.get(tt, t_ftm), tt)
            P.emit("vector", lambda e, ft=ft, vtt=vtt, c0=c0: e.tensor_scalar(
                vtt, ft, stt[:, c0 + 2:c0 + 3], stt[:, c0 + 4:c0 + 5],
                ALU.subtract, ALU.mult),
                waits=[t_rstd2, P.tok("vector"), tk("vt")], inc=1)
            t_t1 = P.tok("vector")
            P.emit("gpsimd", lambda e, vtt=vtt: e.tensor_tensor(
                vtt, vtt, gam[:], ALU.mult),
                waits=[t_t1, t_const], inc=1)
            P.emit("gpsimd", lambda e, vtt=vtt: e.tensor_tensor(
                vtt, vtt, bet[:], ALU.add), inc=1)
            t_gbs[tt] = P.tok("gpsimd")

        def s8_ln1(tt):
            xt = xres[:, tt * D:(tt + 1) * D]
            vtt = vt[:, tt * D:(tt + 1) * D]
            P.emit("vector", lambda e, xt=xt, vtt=vtt: e.tensor_tensor(
                xt, xt, vtt, ALU.add),
                waits=[t_gbs[tt]], inc=1)
            t_x1t = P.tok("vector")
            c1 = 16 + (tt % 2) * 8
            t_rstd1 = ln_stats(xt, c1, t_x1t, 8 + tt)
            P.emit("vector", lambda e, xt=xt, vtt=vtt, c1=c1: e.tensor_scalar(
                vtt, xt, stt[:, c1 + 2:c1 + 3], stt[:, c1 + 4:c1 + 5],
                ALU.subtract, ALU.mult),
                waits=[t_rstd1, P.tok("vector")], inc=1)
            tile_transpose(tt, P.tok("vector"))

        for tt in range(NTT):
            s8_ln2(tt)
            if tt >= 1:
                s8_ln1(tt - 1)
        s8_ln1(NTT - 1)
        t_x1 = P.tok("vector")
        last["fused"] = t_x1
        t_ln1fm = P.tok("vector")
        t_ln1 = t_ln1fm
        last["lnfm"] = t_ln1fm
        last["vt"] = P.tok("tensor")
        last["qk"] = P.tok("vector")
        dbg_tap("ln1fm", lnfm, 8192, c)
        maybe_trunc("s9")

        for tt in range(NTT):
            pg = fb[tt % 2]
            pgn = "f%d" % (tt % 2)
            for dc in range(8):
                P.emit("tensor", lambda e, pg=pg, tt=tt, dc=dc: e.matmul(
                    pg[:, 0:E], lnfm[:, dc * 1024 + tt * 128: dc * 1024 + tt * 128 + 128],
                    wg_sb[:, dc * 8:dc * 8 + 8], start=(dc == 0), stop=False),
                    waits=[t_ln1fm, tk(pgn), t_smallcast] if dc == 0 else ())
            P.emit("tensor", lambda e, pg=pg: e.matmul(
                pg[:, 0:E], onesk1(), rowc[0:1, R_BG:R_BG + E], start=False, stop=True), inc=1)
            t_g = P.tok("tensor")
            P.emit("scalar", lambda e, pg=pg, tt=tt: e.activation(
                gst[:, tt * 8:tt * 8 + 8], pg[:, 0:E], AF.Copy),
                waits=[t_g, tk("gst")], inc=1)
            last[pgn] = P.tok("scalar")
        t_logits = P.tok("scalar")
        P.emit("scalar", lambda e: e.activation(gst[:, 64:128], gst[:, 0:64], AF.Exp),
               waits=[t_logits], inc=1)
        t_ge = P.tok("scalar")
        P.emit("vector", lambda e: e.tensor_reduce(
            gst[:, 128:136], gst[:, 64:128].rearrange("p (a b) -> p a b", b=8), AX.X, ALU.add),
            waits=[t_ge], inc=1)
        P.emit("vector", lambda e: e.reciprocal(gst[:, 136:144], gst[:, 128:136]),
               waits=[P.tok("vector")], inc=1)
        t_rc = P.tok("vector")
        for tt in range(NTT):
            P.emit("vector", lambda e, tt=tt: e.tensor_single_scalar(
                gst[:, 64 + tt * 8:64 + tt * 8 + 8], gst[:, 64 + tt * 8:64 + tt * 8 + 8],
                gst[:, 136 + tt:137 + tt], ALU.mult),
                waits=[t_rc], inc=1)
        t_probs = P.tok("vector")
        P.emit("vector", lambda e: e.tensor_reduce(
            gst[:, 144:152], gst[:, 64:128].rearrange("p (a b) -> p a b", b=8), AX.X, ALU.max),
            waits=[t_probs], inc=1)
        for tt in range(NTT):
            P.emit("vector", lambda e, tt=tt: e.tensor_single_scalar(
                gst[:, 160:168], gst[:, 64 + tt * 8:64 + tt * 8 + 8],
                gst[:, 144 + tt:145 + tt], ALU.is_equal),
                waits=[P.tok("vector")], inc=1)
            P.emit("vector", lambda e: e.tensor_tensor(
                gst[:, 160:168], gst[:, 160:168], iot[:, 8:16], ALU.mult),
                waits=[P.tok("vector")], inc=1)
            P.emit("vector", lambda e: e.tensor_single_scalar(
                gst[:, 160:168], gst[:, 160:168], 8.0, ALU.add),
                waits=[P.tok("vector")], inc=1)
            P.emit("vector", lambda e, tt=tt: e.tensor_reduce(
                gst[:, 152 + tt:153 + tt], gst[:, 160:168], AX.X, ALU.min),
                waits=[P.tok("vector")], inc=1)
            P.emit("vector", lambda e, tt=tt: e.tensor_single_scalar(
                gst[:, 160:168], iot[:, 0:8], gst[:, 152 + tt:153 + tt], ALU.is_equal),
                waits=[P.tok("vector")], inc=1)
            P.emit("vector", lambda e, tt=tt: e.tensor_tensor(
                gst[:, 192 + tt * 8:192 + tt * 8 + 8], gst[:, 160:168],
                gst[:, 64 + tt * 8:64 + tt * 8 + 8], ALU.mult),
                waits=[P.tok("vector")], inc=1)
        t_coef = P.tok("vector")
        last["gst"] = t_coef
        dbg_tap("gates", gst, 256, c)
        maybe_trunc("gate")

        t_w1pre = t_w1l
        t_w2pre = t_w2l
        for ex in range(E):
            t_w1_cur = t_w1pre
            t_w2_cur = t_w2pre
            gi = 0
            for hc in range(NHC):
                b0 = (gi % 2) * 2
                pa, pc = fb[b0], fb[b0 + 1]
                pan, pcn = "f%d" % b0, "f%d" % (b0 + 1)
                wl = ([t_w1_cur, t_ln1fm, tk(pan), tk(pcn)] if gi < 2
                      else [tk(pan), tk(pcn)])
                for dc in range(8):
                    P.emit("tensor", lambda e, pa=pa, dc=dc, hc=hc: e.matmul(
                        pa[:],
                        wbuf[:, dc * 2048 + hc * 128: dc * 2048 + hc * 128 + 128],
                        lnfm[:, dc * 1024: dc * 1024 + 512],
                        start=(dc == 0), stop=(dc == 7)),
                        waits=wl if dc == 0 else ())
                    P.emit("tensor", lambda e, pc=pc, dc=dc, hc=hc: e.matmul(
                        pc[:],
                        wbuf[:, dc * 2048 + hc * 128: dc * 2048 + hc * 128 + 128],
                        lnfm[:, dc * 1024 + 512: dc * 1024 + 1024],
                        start=(dc == 0), stop=(dc == 7)),
                        inc=(1 if dc == 7 else 0))
                t_mm = P.tok("tensor")
                P.emit("scalar", lambda e, pa=pa, hc=hc, ex=ex: e.activation(
                    qk_h[:, hc * 1024: hc * 1024 + 512], pa[:],
                    AF.Gelu, bias=b1c[:, ex * 16 + hc: ex * 16 + hc + 1]),
                    waits=[t_mm, tk("qk")], inc=1)
                last[pan] = P.tok("scalar")
                P.emit("scalar", lambda e, pc=pc, hc=hc, ex=ex: e.activation(
                    qk_h[:, hc * 1024 + 512: hc * 1024 + 1024], pc[:],
                    AF.Gelu, bias=b1c[:, ex * 16 + hc: ex * 16 + hc + 1]),
                    inc=1)
                last[pcn] = P.tok("scalar")
                gi += 1
            t_h = P.tok("scalar")
            t_hmm = P.tok("tensor")
            if ex + 1 < E:
                P.emit("sync", lambda e, ex=ex: e.dma_start(
                    out=wbuf[:, 0:16384].rearrange("p (n k) -> p n k", k=H),
                    in_=w1s[ex + 1].rearrange("(n p) k -> p n k", p=128)),
                    waits=[t_w1c[ex + 1], t_hmm], inc=16)
                t_w1pre = P.tok("sync")
            ybpairs = [(4, 5), (0, 1)]
            gi = 0
            for tt in range(NTT):
                b0, b1_ = ybpairs[gi % 2]
                pa, pc = fb[b0], fb[b1_]
                pan, pcn = "f%d" % b0, "f%d" % b1_
                wl = ([t_h, t_w2_cur, tk(pan), tk(pcn)] if gi < 2
                      else [tk(pan), tk(pcn)])
                gi += 1
                for hc in range(NHC):
                    P.emit("tensor", lambda e, pa=pa, hc=hc, tt=tt: e.matmul(
                        pa[:],
                        qk_h[:, hc * 1024 + tt * 128: hc * 1024 + tt * 128 + 128],
                        wbuf[:, 16384 + hc * 1024: 16384 + hc * 1024 + 512],
                        start=(hc == 0), stop=False),
                        waits=wl if hc == 0 else ())
                    P.emit("tensor", lambda e, pc=pc, hc=hc, tt=tt: e.matmul(
                        pc[:],
                        qk_h[:, hc * 1024 + tt * 128: hc * 1024 + tt * 128 + 128],
                        wbuf[:, 16384 + hc * 1024 + 512: 16384 + hc * 1024 + 1024],
                        start=(hc == 0), stop=False))
                P.emit("tensor", lambda e, pa=pa: e.matmul(
                    pa[:], onesk1(), rows2[0:1, 0:512],
                    start=False, stop=True))
                P.emit("tensor", lambda e, pc=pc: e.matmul(
                    pc[:], onesk1(), rows2[0:1, 512:1024],
                    start=False, stop=True), inc=1)
                t_mm = P.tok("tensor")
                for dh, pb2, pbn2 in [(0, pa, pan), (1, pc, pcn)]:
                    P.emit("vector", lambda e, pb2=pb2, tt=tt, dh=dh, ex=ex: e.scalar_tensor_tensor(
                        xres[:, tt * 1024 + dh * 512: tt * 1024 + dh * 512 + 512], pb2[:],
                        gst[:, 192 + tt * 8 + ex: 192 + tt * 8 + ex + 1],
                        xres[:, tt * 1024 + dh * 512: tt * 1024 + dh * 512 + 512],
                        ALU.mult, ALU.add),
                        waits=([t_mm, t_coef] if dh == 0 else ()), inc=1)
                    last[pbn2] = P.tok("vector")
            t_ymm = P.tok("tensor")
            last["qk"] = t_ymm
            last["rows2"] = t_ymm
            if ex + 1 < E:
                P.emit("sync", lambda e, ex=ex: e.dma_start(
                    out=wbuf[:, 16384:32768].rearrange("p (n k) -> p n k", k=D),
                    in_=w2s[ex + 1, 0:H, :].rearrange("(n p) k -> p n k", p=128)),
                    waits=[t_w2c[ex + 1], t_ymm], inc=16)
                P.emit("sync", lambda e, ex=ex: e.dma_start(
                    out=rows2[0:1, 0:D], in_=w2s[ex + 1, H:H + 1, :]),
                    waits=[t_ymm], inc=16)
                t_w2pre = P.tok("sync")
        last["wb1"] = P.tok("tensor")
        last["wb2"] = P.tok("tensor")
        last["wb3"] = P.tok("tensor")
        t_ffn = P.tok("vector")
        last["gst"] = t_ffn

        P.emit("sync", lambda e, c=c: e.dma_start(
            out=out_h[c].rearrange("(a p) d -> p a d", p=128),
            in_=xres[:].rearrange("p (a d) -> p a d", d=D)),
            waits=[t_ffn], inc=16)
        last["xres"] = P.tok("sync")

    try:
        for c in range(n_chunks):
            emit_chunk(c)
    except _Trunc:
        pass

    P.emit("sync", lambda e: e.nop(), waits=[P.tok("sync")])

    P.build()
    ctx.close()
    return nc


def _host_prep(inputs):
    asc = np.ascontiguousarray
    f32 = np.float32
    bf = ml_dtypes.bfloat16
    d = {}
    d["pt_l"] = asc(np.asarray(inputs["L0_attn_proj_local"], dtype=f32).T)
    d["pt_g"] = asc(np.asarray(inputs["L0_attn_proj_global"], dtype=f32).T)
    d["wint_l"] = asc(np.asarray(inputs["L0_attn_Wl_in"], dtype=f32).T)
    d["wint_g"] = asc(np.asarray(inputs["L0_attn_Wg_in"], dtype=f32).T)
    d["wot_l"] = asc(np.asarray(inputs["L0_attn_Wl_out"], dtype=f32).T)
    d["wot_g"] = asc(np.asarray(inputs["L0_attn_Wg_out"], dtype=f32).T)
    d["w1"] = asc(np.asarray(inputs["L0_ffn_W1"], dtype=f32))
    d["w2"] = asc(np.asarray(inputs["L0_ffn_W2"], dtype=f32))
    d["b2"] = asc(np.asarray(inputs["L0_ffn_b2"], dtype=f32))
    for name, key in [("l", "L0_attn_bl_in"), ("g", "L0_attn_bg_in")]:
        b = np.asarray(inputs[key], dtype=f32)
        d["bqk_" + name] = asc(b[0:2048].reshape(16, 128).T)
        d["bv_" + name] = asc(b[2048:3072].reshape(1, 1024))
    d["bout_l"] = asc(np.asarray(inputs["L0_attn_bl_out"], dtype=f32).reshape(1, 1024))
    d["bout_g"] = asc(np.asarray(inputs["L0_attn_bg_out"], dtype=f32).reshape(1, 1024))
    d["gam"] = asc(np.broadcast_to(np.asarray(inputs["L0_attn_ln_gamma"], dtype=f32), (128, D)))
    d["bet"] = asc(np.broadcast_to(np.asarray(inputs["L0_attn_ln_beta"], dtype=f32), (128, D)))
    d["fus"] = asc(np.broadcast_to(np.asarray(inputs["L0_attn_fusion_w"], dtype=f32), (128, 2)))
    d["wgg"] = asc(np.asarray(inputs["L0_ffn_Wg"], dtype=f32).reshape(8, 128, 8).transpose(1, 0, 2).reshape(128, 64))
    d["bgg"] = asc(np.asarray(inputs["L0_ffn_bg"], dtype=f32).reshape(1, E))
    d["b1c"] = asc(np.asarray(inputs["L0_ffn_b1"], dtype=f32).reshape(8, 16, 128).transpose(2, 0, 1).reshape(128, 128))
    d["identc"] = np.eye(128, dtype=bf)
    st = np.zeros((128, 128), dtype=bf)
    st[:, 0:64] = 1
    sbm = np.zeros((128, 128), dtype=bf)
    sbm[:, 64:128] = 1
    d["seltc"] = st
    d["selbc"] = sbm
    io = np.zeros((128, 24), dtype=f32)
    io[:, 0:8] = np.arange(8, dtype=f32)
    io[:, 8:16] = np.arange(8, dtype=f32) - 8.0
    io[:, 16] = EPS
    d["iotac"] = io
    d["onesbc"] = np.ones((1, 640), dtype=bf)
    tril = np.tril(np.ones((128, 128), dtype=np.float64), -1) * 8.0
    d["tril8bc"] = tril.astype(bf)
    u1 = np.zeros((128, 128)); u1[64:128, 0:64] = np.eye(64)
    v1 = np.zeros((128, 128)); v1[64:128, :] = tril[0:64, :]
    u2 = np.zeros((128, 128)); u2[64:128, 64:128] = np.eye(64)
    v2 = np.zeros((128, 128)); v2[64:128, :] = tril[64:128, :]
    d["mu1c"] = u1.astype(bf)
    d["mv1c"] = v1.astype(bf)
    d["mu2c"] = u2.astype(bf)
    d["mv2c"] = v2.astype(bf)
    return d


def kernel(**inputs):
    x = np.asarray(inputs["L0_attn_x"], dtype=np.float32)
    B, S, _ = x.shape
    chunks = x.reshape(B * S // T, T, D)
    n_per = chunks.shape[0] // NCORES
    weights = _host_prep(inputs)

    nc = build(n_chunks=n_per)
    in_maps = []
    for ci in range(NCORES):
        m = dict(weights)
        m["x"] = np.ascontiguousarray(chunks[ci * n_per:(ci + 1) * n_per])
        in_maps.append(m)
    res = run_bass_kernel_spmd(nc, in_maps, core_ids=list(range(NCORES)))
    out = np.concatenate([r["out"] for r in res.results], axis=0)
    return out.reshape(B, S, D)


# version 08
# speedup vs baseline: 1.1724x; 1.0076x over previous
import numpy as np
import ml_dtypes
from contextlib import ExitStack
import concourse.bass as bass
import concourse.mybir as mybir
from concourse.bass_utils import run_bass_kernel_spmd

F32 = mybir.dt.float32
BF16 = mybir.dt.bfloat16
AF = mybir.ActivationFunctionType
ALU = mybir.AluOpType
AX = mybir.AxisListType

NCORES = 8
D = 1024
T = 1024
NCHUNK = 8
NTT = 8
NPAIR = 8
NKT = 8
H, NHC = 2048, 16
E = 8
EPS = 1e-5


class Prog:
    ENGS = ["sync", "gpsimd", "tensor", "vector", "scalar"]

    def __init__(self, nc, ctx):
        self.nc = nc
        self.sems = {e: ctx.enter_context(nc.semaphore("S" + e)) for e in self.ENGS}
        self.cnt = {e: 0 for e in self.ENGS}
        self.ops = {e: [] for e in self.ENGS}

    def emit(self, e, fn, waits=(), inc=0):
        ws = tuple((we, wv) for (we, wv) in waits if we is not None and wv > 0)
        self.ops[e].append((fn, ws, inc))
        if inc:
            self.cnt[e] += inc
        return (e, self.cnt[e])

    def tok(self, e):
        return (e, self.cnt[e])

    def build(self):
        with self.nc.Block() as block:
            for e in self.ENGS:
                dec = getattr(block, e)

                def _section(eng, _e=e):
                    seen = {}
                    for (fn, ws, inc) in self.ops[_e]:
                        for (we, wv) in ws:
                            if seen.get(we, 0) >= wv:
                                continue
                            eng.wait_ge(self.sems[we], wv)
                            seen[we] = wv
                        ins = fn(eng)
                        if inc:
                            ins.then_inc(self.sems[_e], inc)
                dec(_section)


def build(n_chunks=NCHUNK, dbg=None, trunc=None):
    nc = bass.Bass("TRN2", target_bir_lowering=False)
    dram = lambda name, shape, dt, kind: nc.dram_tensor(name, shape, dt, kind=kind).ap()

    x_h = dram("x", [n_chunks, T, D], F32, "ExternalInput")
    out_h = dram("out", [n_chunks, T, D], F32, "ExternalOutput")
    pt_l = dram("pt_l", [D, D], F32, "ExternalInput")
    pt_g = dram("pt_g", [D, D], F32, "ExternalInput")
    wint_l = dram("wint_l", [D, 3 * D], F32, "ExternalInput")
    wint_g = dram("wint_g", [D, 3 * D], F32, "ExternalInput")
    wot_l = dram("wot_l", [D, D], F32, "ExternalInput")
    wot_g = dram("wot_g", [D, D], F32, "ExternalInput")
    w1_h = dram("w1", [E, D, H], F32, "ExternalInput")
    w2_h = dram("w2", [E, H, D], F32, "ExternalInput")
    b2_h = dram("b2", [E, D], F32, "ExternalInput")
    bqk_l_h = dram("bqk_l", [128, 16], F32, "ExternalInput")
    bqk_g_h = dram("bqk_g", [128, 16], F32, "ExternalInput")
    bv_l_h = dram("bv_l", [1, D], F32, "ExternalInput")
    bv_g_h = dram("bv_g", [1, D], F32, "ExternalInput")
    bout_l_h = dram("bout_l", [1, D], F32, "ExternalInput")
    bout_g_h = dram("bout_g", [1, D], F32, "ExternalInput")
    gam_h = dram("gam", [128, D], F32, "ExternalInput")
    bet_h = dram("bet", [128, D], F32, "ExternalInput")
    fus_h = dram("fus", [128, 2], F32, "ExternalInput")
    wg_h = dram("wgg", [128, 64], F32, "ExternalInput")
    bg_h = dram("bgg", [1, E], F32, "ExternalInput")
    b1c_h = dram("b1c", [128, 128], F32, "ExternalInput")
    ident_h = dram("identc", [128, 128], BF16, "ExternalInput")
    selt_h = dram("seltc", [128, 128], BF16, "ExternalInput")
    selb_h = dram("selbc", [128, 128], BF16, "ExternalInput")
    iota_h = dram("iotac", [128, 24], F32, "ExternalInput")
    onesb_h = dram("onesbc", [1, 640], BF16, "ExternalInput")
    tril8b_h = dram("tril8bc", [128, 128], BF16, "ExternalInput")
    mu1_h = dram("mu1c", [128, 128], BF16, "ExternalInput")
    mv1_h = dram("mv1c", [128, 128], BF16, "ExternalInput")
    mu2_h = dram("mu2c", [128, 128], BF16, "ExternalInput")
    mv2_h = dram("mv2c", [128, 128], BF16, "ExternalInput")

    dbg_h = dram("dbgout", [128, 16384], F32, "ExternalOutput") if dbg else None
    wints_l = dram("wints_l", [D, 3 * D], BF16, "Internal")
    wints_g = dram("wints_g", [D, 3 * D], BF16, "Internal")
    m_l = dram("m_l", [D, 3 * D], BF16, "Internal")
    m_g = dram("m_g", [D, 3 * D], BF16, "Internal")
    wos_l = dram("wos_l", [D, D], BF16, "Internal")
    wos_g = dram("wos_g", [D, D], BF16, "Internal")
    w1s = dram("w1s", [E, D, H], BF16, "Internal")
    w2s = dram("w2s", [E, H + 1, D], BF16, "Internal")

    ctx = ExitStack()
    sb = lambda name, shape, dt: ctx.enter_context(nc.sbuf_tensor(name, shape, dt))
    psum = lambda name, shape, dt: ctx.enter_context(nc.psum_tensor(name, shape, dt))

    xres = sb("xres", [128, NTT * D], F32)
    lnfm = sb("lnfm", [128, 8 * T], BF16)
    qk_h = sb("qk_h", [128, 16384], BF16)
    vt = sb("vt", [128, NTT * D], BF16)
    fused = sb("fusedb", [128, 8 * T], BF16)
    expt = sb("expt", [128, 2048], BF16)
    wbuf = sb("wbuf", [128, 32768], BF16)
    scr = sb("scr", [128, 1024], F32)
    gst = sb("gst", [128, 256], F32)
    stt = sb("stt", [128, 64], F32)
    gam = sb("gamS", [128, D], F32)
    bet = sb("betS", [128, D], F32)
    fusw = sb("fuswS", [128, 4], F32)
    bqkl = sb("bqklS", [128, 16], F32)
    bqkg = sb("bqkgS", [128, 16], F32)
    b1c = sb("b1cS", [128, 128], F32)
    wg_sb = sb("wgS", [128, 64], BF16)
    iot = sb("iotS", [128, 24], F32)
    ident = sb("identS", [128, 128], BF16)
    selt = sb("seltS", [128, 128], BF16)
    selb = sb("selbS", [128, 128], BF16)
    tril8b = sb("tril8bS", [128, 128], BF16)
    mu1 = sb("mu1S", [128, 128], BF16)
    mv1 = sb("mv1S", [128, 128], BF16)
    mu2 = sb("mu2S", [128, 128], BF16)
    mv2 = sb("mv2S", [128, 128], BF16)
    rowc = sb("rowcS", [128, 4864], BF16)
    rows2 = sb("rows2S", [128, 1024], BF16)
    R_BVL, R_BVG, R_BOL, R_BOG, R_BG, R_ON512, R_ONK1 = 0, 1024, 2048, 3072, 4096, 4104, 4616

    fb = [psum("f%d" % i, [128, 512], F32) for i in range(6)]
    ptrs = [psum("ptrb%d" % i, [128, 1024], BF16) for i in range(2)]

    P = Prog(nc, ctx)
    last = {}

    def tk(name):
        return last.get(name, (None, 0))

    def ones512():
        return rowc[0:1, R_ON512:R_ON512 + 512]

    def onesk1():
        return rowc[0:1, R_ONK1:R_ONK1 + 128]

    def dbg_tap(name, buf, cols, chunk):
        if dbg is None or dbg != name or chunk != 0:
            return
        wl = [P.tok(e) for e in Prog.ENGS]
        t_tap = P.emit("gpsimd", lambda e: e.dma_start(out=dbg_h[:, 0:cols], in_=buf[:, 0:cols]),
                       waits=wl, inc=16)
        for e2 in Prog.ENGS:
            P.emit(e2, lambda e: e.nop(), waits=[t_tap])

    for src, dst in [(gam_h, gam), (bet_h, bet), (bqk_l_h, bqkl), (bqk_g_h, bqkg),
                     (b1c_h, b1c), (iota_h, iot), (ident_h, ident),
                     (selt_h, selt), (selb_h, selb), (tril8b_h, tril8b),
                     (mu1_h, mu1), (mv1_h, mv1), (mu2_h, mu2), (mv2_h, mv2)]:
        P.emit("sync", lambda e, s=src, d=dst: e.dma_start(out=d[:], in_=s), inc=16)
    P.emit("sync", lambda e: e.dma_start(out=fusw[:, 0:2], in_=fus_h), inc=16)
    P.emit("sync", lambda e: e.dma_start(out=rowc[0:1, R_ON512:R_ONK1 + 128], in_=onesb_h), inc=16)
    t_const = P.tok("sync")

    P.emit("gpsimd", lambda e: e.dma_start(out=wg_sb[:], in_=wg_h), inc=16)
    for src, dst in [(bv_l_h, rowc[0:1, R_BVL:R_BVL + D]), (bv_g_h, rowc[0:1, R_BVG:R_BVG + D]),
                     (bout_l_h, rowc[0:1, R_BOL:R_BOL + D]), (bout_g_h, rowc[0:1, R_BOG:R_BOG + D]),
                     (bg_h, rowc[0:1, R_BG:R_BG + E])]:
        P.emit("gpsimd", lambda e, s=src, d=dst: e.dma_start(out=d, in_=s), inc=16)
    t_smallcast = P.tok("gpsimd")

    P.emit("gpsimd", lambda e: e.dma_start(out=wints_l, in_=wint_l), inc=16)
    t_wints = {0: P.tok("gpsimd")}
    P.emit("gpsimd", lambda e: e.dma_start(out=wints_g, in_=wint_g), inc=16)
    t_wints[1] = P.tok("gpsimd")
    P.emit("gpsimd", lambda e: e.dma_start(out=wos_l, in_=wot_l), inc=16)
    P.emit("gpsimd", lambda e: e.dma_start(out=wos_g, in_=wot_g), inc=16)
    t_wos = P.tok("gpsimd")
    t_w1c = [None] * E
    t_w2c = [None] * E
    for e_ in range(E):
        P.emit("gpsimd", lambda e, i=e_: e.dma_start(out=w1s[i], in_=w1_h[i]), inc=16)
        t_w1c[e_] = P.tok("gpsimd")
        P.emit("gpsimd", lambda e, i=e_: e.dma_start(out=w2s[i, 0:H, :], in_=w2_h[i]), inc=16)
        P.emit("gpsimd", lambda e, i=e_: e.dma_start(
            out=w2s[i, H:H + 1, :], in_=b2_h[i:i + 1, :]), inc=16)
        t_w2c[e_] = P.tok("gpsimd")

    P.emit("scalar", lambda e: e.activation(stt[:, 0:2], fusw[:, 0:2], AF.Exp),
           waits=[t_const], inc=1)
    t_fx = P.tok("scalar")
    P.emit("vector", lambda e: e.tensor_reduce(stt[:, 2:3], stt[:, 0:2], AX.X, ALU.add),
           waits=[t_fx], inc=1)
    P.emit("vector", lambda e: e.reciprocal(stt[:, 3:4], stt[:, 2:3]),
           waits=[P.tok("vector")], inc=1)
    P.emit("vector", lambda e: e.tensor_single_scalar(fusw[:, 2:4], stt[:, 0:2], stt[:, 3:4], ALU.mult),
           waits=[P.tok("vector")], inc=1)
    t_fusw = P.tok("vector")
    w0col = fusw[:, 2:3]
    w1col = fusw[:, 3:4]

    P.emit("sync", lambda e: e.dma_start(
        out=xres[:].rearrange("p (a d) -> p a d", d=D),
        in_=x_h[0].rearrange("(a p) d -> p a d", p=128)), inc=16)
    t_x0 = P.tok("sync")

    t_mready = {}
    for which, pt_src, wints_s, m_dst in [(0, pt_l, wints_l, m_l), (1, pt_g, wints_g, m_g)]:
        P.emit("gpsimd", lambda e, s=pt_src: e.dma_start(
            out=wbuf[:, 0:8192].rearrange("p (n k) -> p n k", k=1024),
            in_=s.rearrange("(n p) k -> p n k", p=128)),
            waits=[tk("wb1")], inc=16)
        t_pt = P.tok("gpsimd")
        P.emit("sync", lambda e, s=wints_s: e.dma_start(
            out=wbuf[:, 8192:32768].rearrange("p (n k) -> p n k", k=3072),
            in_=s.rearrange("(n p) k -> p n k", p=128)),
            waits=[t_wints[which], tk("wb2"), tk("wb3")], inc=16)
        t_wi = P.tok("sync")
        for ac in range(8):
            slot = ac % 2
            for fh in range(6):
                pb = fb[fh % 4]
                pbn = "f%d" % (fh % 4)
                for bc in range(8):
                    P.emit("tensor", lambda e, pb=pb, bc=bc, ac=ac, fh=fh: e.matmul(
                        pb[:], wbuf[:, bc * 1024 + ac * 128: bc * 1024 + ac * 128 + 128],
                        wbuf[:, 8192 + bc * 3072 + fh * 512: 8192 + bc * 3072 + fh * 512 + 512],
                        start=(bc == 0), stop=(bc == 7)),
                        waits=[t_pt, t_wi, tk(pbn)] if bc == 0 else (),
                        inc=(1 if bc == 7 else 0))
                t_mm = P.tok("tensor")
                P.emit("scalar", lambda e, pb=pb, slot=slot, fh=fh: e.activation(
                    vt[:, slot * 3072 + fh * 512: slot * 3072 + fh * 512 + 512], pb[:], AF.Copy),
                    waits=[t_mm, tk("vts%d" % slot)], inc=1)
                last[pbn] = P.tok("scalar")
            t_ev = P.tok("scalar")
            P.emit("sync", lambda e, ac=ac, slot=slot, d=m_dst: e.dma_start(
                out=d[ac * 128:(ac + 1) * 128, :], in_=vt[:, slot * 3072:(slot + 1) * 3072]),
                waits=[t_ev], inc=16)
            last["vts%d" % slot] = P.tok("sync")
        last["wb1"] = P.tok("tensor")
        last["wb2"] = P.tok("tensor")
        last["wb3"] = P.tok("tensor")
        last["vt"] = P.tok("sync")
        t_mready[which] = P.tok("sync")


    def ln_stats(src_tile, c0, src_tok, sqslot):
        sq = qk_h[:, sqslot * 1024:(sqslot + 1) * 1024]
        P.emit("scalar", lambda e: e.activation(sq, src_tile, AF.Square),
               waits=[src_tok, tk("qk"), tk("lnsq%d" % sqslot)], inc=1)
        t_sq = P.tok("scalar")
        P.emit("vector", lambda e: e.tensor_reduce(stt[:, c0:c0 + 1], src_tile, AX.X, ALU.add),
               waits=[src_tok, tk("sttc%d" % c0)], inc=1)
        P.emit("vector", lambda e: e.tensor_reduce(stt[:, c0 + 1:c0 + 2], sq, AX.X, ALU.add),
               waits=[t_sq], inc=1)
        last["lnsq%d" % sqslot] = P.tok("vector")
        P.emit("vector", lambda e: e.tensor_single_scalar(
            stt[:, c0 + 2:c0 + 3], stt[:, c0:c0 + 1], 1.0 / D, ALU.mult),
            waits=[P.tok("vector")], inc=1)
        P.emit("vector", lambda e: e.tensor_tensor(
            stt[:, c0 + 5:c0 + 6], stt[:, c0 + 2:c0 + 3], stt[:, c0 + 2:c0 + 3], ALU.mult),
            waits=[P.tok("vector")], inc=1)
        P.emit("vector", lambda e: e.tensor_single_scalar(
            stt[:, c0 + 6:c0 + 7], stt[:, c0 + 1:c0 + 2], 1.0 / D, ALU.mult),
            waits=[P.tok("vector")], inc=1)
        P.emit("vector", lambda e: e.tensor_tensor(
            stt[:, c0 + 3:c0 + 4], stt[:, c0 + 6:c0 + 7], stt[:, c0 + 5:c0 + 6], ALU.subtract),
            waits=[P.tok("vector")], inc=1)
        t_var = P.tok("vector")
        P.emit("scalar", lambda e: e.activation(stt[:, c0 + 5:c0 + 6], stt[:, c0 + 3:c0 + 4],
                                                AF.Ln, bias=iot[:, 16:17]), waits=[t_var], inc=1)
        P.emit("scalar", lambda e: e.activation(stt[:, c0 + 4:c0 + 5], stt[:, c0 + 5:c0 + 6],
                                                AF.Exp, scale=-0.5),
               waits=[P.tok("scalar")], inc=1)
        last["sttc%d" % c0] = P.tok("scalar")
        return P.tok("scalar")

    def tile_transpose(tt, t_ts):
        pb = ptrs[tt % 2]
        pn = "ptr%d" % (tt % 2)
        t_prev_ev = tk(pn)
        for j in range(8):
            P.emit("tensor", lambda e, i=tt, j=j, pb=pb: e.transpose(
                pb[:, j * 128:(j + 1) * 128],
                vt[:, i * 1024 + j * 128: i * 1024 + j * 128 + 128], ident[:]),
                waits=[t_ts, t_prev_ev] if j == 0 else (), inc=(1 if j == 7 else 0))
        t_tr = P.tok("tensor")
        dst = lnfm[:].rearrange("p (j k) -> p j k", k=1024)[:, :, tt * 128:(tt + 1) * 128]
        srcv = pb[:].rearrange("p (j k) -> p j k", k=128)
        P.emit("vector", lambda e, dst=dst, srcv=srcv: e.tensor_copy(dst, srcv),
               waits=[t_tr, tk("lnfm")], inc=1)
        last[pn] = P.tok("vector")

    def emit_sums_av(i, h01, kt, t_e, sm, smn, ub, ubn, sel, hh, ec):
        P.emit("tensor", lambda e, ub=ub, hh=hh, kt=kt, ec=ec, h01=h01: e.matmul(
            ub[h01 * 64:h01 * 64 + 64, :],
            vt[:, kt * 1024 + hh * 64: kt * 1024 + hh * 64 + 64],
            expt[:, ec:ec + 512],
            start=(kt == 0), stop=(kt == NKT - 1)),
            waits=([t_e, tk("vready"), tk(ubn)] if (h01 == 0 and kt == 0) else [t_e]),
            inc=0)
        P.emit("tensor", lambda e, sm=sm, sel=sel, ec=ec, h01=h01, kt=kt: e.matmul(
            sm[:], sel[:], expt[:, ec:ec + 512],
            start=(h01 == 0 and kt == 0), stop=(h01 == 1 and kt == NKT - 1)),
            waits=([tk(smn)] if (h01 == 0 and kt == 0) else ()),
            inc=(1 if (h01 == 1 and kt == NKT - 1) else 0))

    class _Trunc(Exception):
        pass

    def maybe_trunc(point):
        if trunc is not None and trunc == point:
            t_all = [P.tok(e) for e in Prog.ENGS]
            P.emit("sync", lambda e: e.dma_start(
                out=out_h[0].rearrange("(a p) d -> p a d", p=128),
                in_=xres[:].rearrange("p (a d) -> p a d", d=D)),
                waits=t_all, inc=16)
            raise _Trunc()

    def emit_chunk(c):
        if c == 0:
            t_x = t_x0
        else:
            P.emit("sync", lambda e: e.dma_start(
                out=xres[:].rearrange("p (a d) -> p a d", d=D),
                in_=x_h[c].rearrange("(a p) d -> p a d", p=128)),
                waits=[tk("xres")], inc=16)
            t_x = P.tok("sync")

        for tt in range(NTT):
            xt = xres[:, tt * D:(tt + 1) * D]
            c0 = (tt % 4) * 8
            t_rstd = ln_stats(xt, c0, t_x, tt)
            P.emit("vector", lambda e, xt=xt, tt=tt, c0=c0: e.tensor_scalar(
                vt[:, tt * D:(tt + 1) * D], xt, stt[:, c0 + 2:c0 + 3], stt[:, c0 + 4:c0 + 5],
                ALU.subtract, ALU.mult),
                waits=[t_rstd, P.tok("vector"), tk("vt")], inc=1)
            tile_transpose(tt, P.tok("vector"))
        t_lnfm = P.tok("vector")
        last["lnfm"] = t_lnfm
        last["vt"] = P.tok("tensor")
        last["qk"] = P.tok("vector")
        dbg_tap("ln0fm", lnfm, 8192, c)
        maybe_trunc("s2")

        def load_m(which):
            m_src = m_l if which == 0 else m_g
            P.emit("sync", lambda e, s=m_src: e.dma_start(
                out=wbuf[:, 0:24576].rearrange("p (n k) -> p n k", k=3072),
                in_=s.rearrange("(n p) k -> p n k", p=128)),
                waits=[tk("wb1"), tk("wb2"), t_mready[which]], inc=16)
            return P.tok("sync")

        def load_wo(which):
            wo_src = wos_l if which == 0 else wos_g
            P.emit("sync", lambda e, s=wo_src: e.dma_start(
                out=wbuf[:, 24576:32768].rearrange("p (n k) -> p n k", k=1024),
                in_=s.rearrange("(n p) k -> p n k", p=128)),
                waits=[t_wos, tk("wb3")], inc=16)
            return P.tok("sync")

        def qkv(which, t_wload):
            bqk = bqkl if which == 0 else bqkg
            rbv = R_BVL if which == 0 else R_BVG
            gi = 0
            for part in range(2):
                for fc in range(8):
                    b0 = (gi % 2) * 2
                    pa, pc = fb[b0], fb[b0 + 1]
                    pan, pcn = "f%d" % b0, "f%d" % (b0 + 1)
                    wl = ([t_wload, t_lnfm, tk(pan), tk(pcn)] if gi < 2
                          else [tk(pan), tk(pcn)])
                    for dc in range(8):
                        P.emit("tensor", lambda e, pa=pa, dc=dc, fc=fc, part=part: e.matmul(
                            pa[:],
                            wbuf[:, dc * 3072 + part * 1024 + fc * 128: dc * 3072 + part * 1024 + fc * 128 + 128],
                            lnfm[:, dc * 1024: dc * 1024 + 512],
                            start=(dc == 0), stop=(dc == 7)),
                            waits=wl if dc == 0 else ())
                        P.emit("tensor", lambda e, pc=pc, dc=dc, fc=fc, part=part: e.matmul(
                            pc[:],
                            wbuf[:, dc * 3072 + part * 1024 + fc * 128: dc * 3072 + part * 1024 + fc * 128 + 128],
                            lnfm[:, dc * 1024 + 512: dc * 1024 + 1024],
                            start=(dc == 0), stop=(dc == 7)),
                            inc=(1 if dc == 7 else 0))
                    t_mm = P.tok("tensor")
                    dst_c = part * 8192 + fc * 1024
                    P.emit("scalar", lambda e, pa=pa, dst_c=dst_c, bqk=bqk, part=part, fc=fc: e.activation(
                        qk_h[:, dst_c:dst_c + 512], pa[:], AF.Identity,
                        bias=bqk[:, part * 8 + fc:part * 8 + fc + 1]),
                        waits=[t_mm, tk("qk")], inc=1)
                    last[pan] = P.tok("scalar")
                    P.emit("scalar", lambda e, pc=pc, dst_c=dst_c, bqk=bqk, part=part, fc=fc: e.activation(
                        qk_h[:, dst_c + 512:dst_c + 1024], pc[:], AF.Identity,
                        bias=bqk[:, part * 8 + fc:part * 8 + fc + 1]),
                        inc=1)
                    last[pcn] = P.tok("scalar")
                    gi += 1
            for tt in range(8):
                pa, pc = fb[4], fb[5]
                wl = [t_wload, t_lnfm, tk("f4"), tk("f5"), t_smallcast]
                for dc in range(8):
                    P.emit("tensor", lambda e, pa=pa, dc=dc, tt=tt: e.matmul(
                        pa[:],
                        lnfm[:, dc * 1024 + tt * 128: dc * 1024 + tt * 128 + 128],
                        wbuf[:, dc * 3072 + 2048: dc * 3072 + 2048 + 512],
                        start=(dc == 0), stop=False),
                        waits=wl if dc == 0 else ())
                    P.emit("tensor", lambda e, pc=pc, dc=dc, tt=tt: e.matmul(
                        pc[:],
                        lnfm[:, dc * 1024 + tt * 128: dc * 1024 + tt * 128 + 128],
                        wbuf[:, dc * 3072 + 2560: dc * 3072 + 3072],
                        start=(dc == 0), stop=False))
                P.emit("tensor", lambda e, pa=pa, rbv=rbv: e.matmul(
                    pa[:], onesk1(), rowc[0:1, rbv: rbv + 512],
                    start=False, stop=True))
                P.emit("tensor", lambda e, pc=pc, rbv=rbv: e.matmul(
                    pc[:], onesk1(), rowc[0:1, rbv + 512: rbv + 1024],
                    start=False, stop=True), inc=1)
                t_mm = P.tok("tensor")
                P.emit("vector", lambda e, pa=pa, tt=tt: e.tensor_copy(
                    vt[:, tt * 1024: tt * 1024 + 512], pa[:]),
                    waits=[t_mm, tk("vt")], inc=1)
                last["f4"] = P.tok("vector")
                P.emit("vector", lambda e, pc=pc, tt=tt: e.tensor_copy(
                    vt[:, tt * 1024 + 512: tt * 1024 + 1024], pc[:]),
                    inc=1)
                last["f5"] = P.tok("vector")
            last["qkready"] = P.tok("scalar")
            last["vready"] = P.tok("vector")
            last["wb1"] = P.tok("tensor")
            last["wb2"] = P.tok("tensor")

        def attn_out_proj(which):
            rbo = R_BOL if which == 0 else R_BOG
            gi = 0
            for dc in range(8):
                b0 = (gi % 2) * 2
                pa, pc = fb[b0], fb[b0 + 1]
                pan, pcn = "f%d" % b0, "f%d" % (b0 + 1)
                wl = ([tk("oready"), tk(pan), tk(pcn)] if gi < 2 else [tk(pan), tk(pcn)])
                gi += 1
                for fc in range(8):
                    P.emit("tensor", lambda e, pa=pa, fc=fc, dc=dc: e.matmul(
                        pa[:],
                        wbuf[:, 24576 + fc * 1024 + dc * 128: 24576 + fc * 1024 + dc * 128 + 128],
                        qk_h[:, fc * 1024: fc * 1024 + 512],
                        start=(fc == 0), stop=False),
                        waits=wl if fc == 0 else ())
                    P.emit("tensor", lambda e, pc=pc, fc=fc, dc=dc: e.matmul(
                        pc[:],
                        wbuf[:, 24576 + fc * 1024 + dc * 128: 24576 + fc * 1024 + dc * 128 + 128],
                        qk_h[:, fc * 1024 + 512: fc * 1024 + 1024],
                        start=(fc == 0), stop=False))
                P.emit("tensor", lambda e, pa=pa, dc=dc, rbo=rbo: e.matmul(
                    pa[:], rowc[0:1, rbo + dc * 128: rbo + dc * 128 + 128], ones512(),
                    start=False, stop=True))
                P.emit("tensor", lambda e, pc=pc, dc=dc, rbo=rbo: e.matmul(
                    pc[:], rowc[0:1, rbo + dc * 128: rbo + dc * 128 + 128], ones512(),
                    start=False, stop=True), inc=1)
                t_mm = P.tok("tensor")
                for th, pb2, pbn2 in [(0, pa, pan), (1, pc, pcn)]:
                    dst = fused[:, dc * 1024 + th * 512: dc * 1024 + th * 512 + 512]
                    if which == 0:
                        P.emit("vector", lambda e, pb2=pb2, dst=dst: e.tensor_single_scalar(
                            dst, pb2[:], w0col, ALU.mult),
                            waits=([t_mm, t_fusw, tk("fused")] if th == 0 else ()), inc=1)
                    else:
                        P.emit("vector", lambda e, pb2=pb2, dst=dst: e.scalar_tensor_tensor(
                            dst, pb2[:], w1col, dst, ALU.mult, ALU.add),
                            waits=([t_mm, t_fusw, tk("fusedacc")] if th == 0 else ()), inc=1)
                    last[pbn2] = P.tok("vector")
            last["fusedacc"] = P.tok("vector")
            last["wb3"] = P.tok("tensor")

        t_wl_m = load_m(0)
        load_wo(0)
        qkv(0, t_wl_m)
        dbg_tap("qk_l", qk_h, 16384, c)
        dbg_tap("vt_l", vt, 8192, c)
        t_wl_mg = load_m(1)
        maybe_trunc("qkv0")

        units = [(w, pr) for w in range(8) for pr in range(NPAIR)]

        def l_scores(i):
            w, pr = units[i]
            ec = (i % 4) * 512
            P.emit("tensor", lambda e, pr=pr, w=w: e.matmul(
                fb[0][:, 0:128],
                qk_h[0:64, 8192 + pr * 1024 + w * 128: 8192 + pr * 1024 + w * 128 + 128],
                qk_h[0:64, pr * 1024 + w * 128: pr * 1024 + w * 128 + 128],
                start=True, stop=False),
                waits=[tk("qkready"), tk("f0")])
            P.emit("tensor", lambda e: e.matmul(
                fb[0][:, 0:128], ident[:], tril8b[:], start=False, stop=True), inc=1)
            t_s0 = P.tok("tensor")
            P.emit("tensor", lambda e, pr=pr, w=w: e.matmul(
                fb[1][:, 0:128],
                qk_h[64:128, 8192 + pr * 1024 + w * 128: 8192 + pr * 1024 + w * 128 + 128],
                qk_h[64:128, pr * 1024 + w * 128: pr * 1024 + w * 128 + 128],
                start=True, stop=False),
                waits=[tk("f1")])
            P.emit("tensor", lambda e: e.matmul(
                fb[1][:, 0:128], mu1[64:128, :], mv1[64:128, :], start=False, stop=False))
            P.emit("tensor", lambda e: e.matmul(
                fb[1][:, 0:128], mu2[64:128, :], mv2[64:128, :], start=False, stop=True), inc=1)
            t_s1 = P.tok("tensor")
            P.emit("scalar", lambda e, ec=ec: e.activation(
                expt[:, ec:ec + 128], fb[0][:, 0:128], AF.Exp, scale=0.125),
                waits=[t_s0], inc=1)
            last["f0"] = P.tok("scalar")
            P.emit("scalar", lambda e, ec=ec: e.activation(
                expt[:, ec + 128:ec + 256], fb[1][:, 0:128], AF.Exp, scale=0.125),
                waits=[t_s1], inc=1)
            last["f1"] = P.tok("scalar")
            return P.tok("scalar")

        def l_tail(i, t_e):
            w, pr = units[i]
            sm = fb[2 + i % 2]
            smn = "f%d" % (2 + i % 2)
            ub = fb[4 + i % 2]
            ubn = "f%d" % (4 + i % 2)
            ec = (i % 4) * 512
            P.emit("tensor", lambda e, sm=sm, ec=ec: e.matmul(
                sm[:, 0:128], selt[:], expt[:, ec:ec + 128],
                start=True, stop=False), waits=[t_e, tk(smn)])
            P.emit("tensor", lambda e, sm=sm, ec=ec: e.matmul(
                sm[:, 0:128], selb[:], expt[:, ec + 128:ec + 256],
                start=False, stop=True), inc=1)
            t_sum = P.tok("tensor")
            P.emit("tensor", lambda e, ub=ub, pr=pr, w=w, ec=ec: e.matmul(
                ub[0:64, 0:128],
                vt[:, w * 1024 + (pr * 2) * 64: w * 1024 + (pr * 2) * 64 + 64],
                expt[:, ec:ec + 128],
                start=True, stop=True), waits=[tk("vready"), tk(ubn)])
            P.emit("tensor", lambda e, ub=ub, pr=pr, w=w, ec=ec: e.matmul(
                ub[64:128, 0:128],
                vt[:, w * 1024 + (pr * 2 + 1) * 64: w * 1024 + (pr * 2 + 1) * 64 + 64],
                expt[:, ec + 128:ec + 256],
                start=True, stop=True), inc=1)
            t_av = P.tok("tensor")
            rc = (i % 2) * 512
            P.emit("vector", lambda e, sm=sm, rc=rc: e.reciprocal(
                scr[:, rc:rc + 128], sm[:, 0:128]),
                waits=[t_sum, tk("scrl%d" % (i % 2))], inc=1)
            t_r = P.tok("vector")
            last[smn] = t_r
            oc = pr * 1024 + w * 128
            P.emit("vector", lambda e, ub=ub, oc=oc, rc=rc: e.tensor_tensor(
                qk_h[0:64, oc:oc + 128], ub[0:64, 0:128], scr[0:64, rc:rc + 128], ALU.mult),
                waits=[t_av, t_r], inc=1)
            P.emit("vector", lambda e, ub=ub, oc=oc, rc=rc: e.tensor_tensor(
                qk_h[64:128, oc:oc + 128], ub[64:128, 0:128], scr[64:128, rc:rc + 128], ALU.mult),
                inc=1)
            last[ubn] = P.tok("vector")
            last["scrl%d" % (i % 2)] = P.tok("vector")

        t_es = {}
        for i in range(len(units)):
            t_es[i] = l_scores(i)
            maybe_trunc("lsc%d" % i)
            if i >= 1:
                l_tail(i - 1, t_es[i - 1])
                maybe_trunc("lu%d" % (i - 1))
        l_tail(len(units) - 1, t_es[len(units) - 1])
        last["oready"] = P.tok("vector")
        dbg_tap("oloc", qk_h, 8192, c)
        maybe_trunc("local")
        attn_out_proj(0)
        last["qk"] = P.tok("vector")
        last["vt"] = P.tok("tensor")
        dbg_tap("fusedl", fused, 8192, c)
        maybe_trunc("outproj0")

        load_wo(1)
        qkv(1, t_wl_mg)
        dbg_tap("qk_g", qk_h, 16384, c)
        dbg_tap("vt_g", vt, 8192, c)
        maybe_trunc("qkv1")
        P.emit("sync", lambda e: e.dma_start(
            out=wbuf[:, 0:16384].rearrange("p (n k) -> p n k", k=H),
            in_=w1s[0].rearrange("(n p) k -> p n k", p=128)),
            waits=[t_w1c[0], tk("wb1"), tk("wb2")], inc=16)
        t_w1l = P.tok("sync")

        gunits = [(pr, qb) for pr in range(NPAIR) for qb in range(2)]
        for i, (pr, qb) in enumerate(gunits):
            sm = fb[2 + i % 2]
            smn = "f%d" % (2 + i % 2)
            ub = fb[4 + i % 2]
            ubn = "f%d" % (4 + i % 2)
            pend = None
            for h01 in range(2):
                hh = pr * 2 + h01
                base = 64 * h01
                sel = selt if h01 == 0 else selb
                for kt in range(NKT):
                    sc = fb[kt % 2]
                    scn = "f%d" % (kt % 2)
                    P.emit("tensor", lambda e, sc=sc, base=base, kt=kt, qb=qb, pr=pr: e.matmul(
                        sc[:],
                        qk_h[base:base + 64, 8192 + pr * 1024 + kt * 128: 8192 + pr * 1024 + kt * 128 + 128],
                        qk_h[base:base + 64, pr * 1024 + qb * 512: pr * 1024 + qb * 512 + 512],
                        start=True, stop=True),
                        waits=[tk("qkready"), tk(scn)], inc=1)
                    t_s = P.tok("tensor")
                    dstart = kt * 128 - qb * 512
                    t_m = t_s
                    if 0 <= dstart < 512:
                        P.emit("vector", lambda e, sc=sc, dstart=dstart: e.tensor_tensor(
                            sc[:, dstart:dstart + 128], sc[:, dstart:dstart + 128],
                            tril8b[:], ALU.add), waits=[t_s], inc=1)
                        t_m = P.tok("vector")
                    ec = (h01 * 2 + kt % 2) * 512
                    wl = [t_m]
                    if dstart >= 512:
                        P.emit("scalar", lambda e, sc=sc, ec=ec: e.activation(
                            expt[:, ec:ec + 512], sc[:], AF.Exp, scale=0.125, bias=1.0),
                            waits=wl, inc=1)
                    elif dstart <= 0:
                        P.emit("scalar", lambda e, sc=sc, ec=ec: e.activation(
                            expt[:, ec:ec + 512], sc[:], AF.Exp, scale=0.125),
                            waits=wl, inc=1)
                    else:
                        P.emit("scalar", lambda e, sc=sc, ec=ec, ds=dstart: e.activation(
                            expt[:, ec:ec + ds], sc[:, 0:ds], AF.Exp, scale=0.125, bias=1.0),
                            waits=wl, inc=1)
                        P.emit("scalar", lambda e, sc=sc, ec=ec, ds=dstart: e.activation(
                            expt[:, ec + ds:ec + 512], sc[:, ds:512], AF.Exp, scale=0.125),
                            inc=1)
                    t_e = P.tok("scalar")
                    last[scn] = t_e
                    if pend is not None:
                        emit_sums_av(*pend)
                    pend = (i, h01, kt, t_e, sm, smn, ub, ubn, sel, hh, ec)
            emit_sums_av(*pend)
            t_gsum = P.tok("tensor")
            rc = (i % 2) * 512
            P.emit("vector", lambda e, sm=sm, rc=rc: e.reciprocal(scr[:, rc:rc + 512], sm[:]),
                   waits=[t_gsum, tk("scrg%d" % (i % 2))], inc=1)
            t_r = P.tok("vector")
            last[smn] = t_r
            oc = pr * 1024 + qb * 512
            P.emit("vector", lambda e, ub=ub, oc=oc, rc=rc: e.tensor_tensor(
                qk_h[0:64, oc:oc + 512], ub[0:64, :], scr[0:64, rc:rc + 512], ALU.mult),
                waits=[t_r], inc=1)
            P.emit("vector", lambda e, ub=ub, oc=oc, rc=rc: e.tensor_tensor(
                qk_h[64:128, oc:oc + 512], ub[64:128, :], scr[64:128, rc:rc + 512], ALU.mult),
                inc=1)
            last[ubn] = P.tok("vector")
            last["scrg%d" % (i % 2)] = P.tok("vector")
        last["oready"] = P.tok("vector")
        dbg_tap("oglo", qk_h, 8192, c)
        maybe_trunc("global")
        attn_out_proj(1)
        last["qk"] = P.tok("vector")
        last["vt"] = P.tok("tensor")
        dbg_tap("fusedfm", fused, 8192, c)
        maybe_trunc("outproj1")

        P.emit("sync", lambda e: e.dma_start(
            out=wbuf[:, 16384:32768].rearrange("p (n k) -> p n k", k=D),
            in_=w2s[0, 0:H, :].rearrange("(n p) k -> p n k", p=128)),
            waits=[t_w2c[0], tk("wb2"), tk("wb3")], inc=16)
        P.emit("sync", lambda e: e.dma_start(
            out=rows2[0:1, 0:D], in_=w2s[0, H:H + 1, :]),
            waits=[tk("rows2")], inc=16)
        t_w2l = P.tok("sync")

        t_fa = tk("fusedacc")
        t_tile = {}
        entries = [(i, j) for i in range(8) for j in range(i, 8)]
        groups, cur, nb = [], [], 0
        for en in entries:
            b = 1 if en[0] == en[1] else 2
            if nb + b > 8:
                groups.append(cur)
                cur, nb = [], 0
            cur.append(en)
            nb += b
        groups.append(cur)
        for gi2, grp in enumerate(groups):
            pb = ptrs[gi2 % 2]
            pn = "ptr%d" % (gi2 % 2)
            t_prev = tk(pn)
            blocks = []
            for (i, j) in grp:
                blocks.append((i, j))
                if i != j:
                    blocks.append((j, i))
            for bi, (a, b) in enumerate(blocks):
                P.emit("tensor", lambda e, a=a, b=b, s=bi, pb=pb: e.transpose(
                    pb[:, s * 128:(s + 1) * 128],
                    fused[:, a * 1024 + b * 128: a * 1024 + b * 128 + 128], ident[:]),
                    waits=[t_fa, t_prev] if bi == 0 else (),
                    inc=(1 if bi == len(blocks) - 1 else 0))
            t_t = P.tok("tensor")
            for bi, (a, b) in enumerate(blocks):
                P.emit("vector", lambda e, a=a, b=b, s=bi, pb=pb: e.tensor_copy(
                    fused[:, b * 1024 + a * 128: b * 1024 + a * 128 + 128], pb[:, s * 128:(s + 1) * 128]),
                    waits=[t_t] if bi == 0 else (), inc=(1 if bi == len(blocks) - 1 else 0))
            last[pn] = P.tok("vector")
            for (i, j) in grp:
                if j == 7:
                    t_tile[i] = P.tok("vector")
        t_ftm = P.tok("vector")
        dbg_tap("fusedg", fused, 8192, c)
        maybe_trunc("s7")

        t_gbs = {}

        def s8_ln2(tt):
            ft = fused[:, tt * D:(tt + 1) * D]
            vtt = vt[:, tt * D:(tt + 1) * D]
            c0 = (tt % 2) * 8
            t_rstd2 = ln_stats(ft, c0, t_tile.get(tt, t_ftm), tt)
            P.emit("vector", lambda e, ft=ft, vtt=vtt, c0=c0: e.tensor_scalar(
                vtt, ft, stt[:, c0 + 2:c0 + 3], stt[:, c0 + 4:c0 + 5],
                ALU.subtract, ALU.mult),
                waits=[t_rstd2, P.tok("vector"), tk("vt")], inc=1)
            t_t1 = P.tok("vector")
            P.emit("gpsimd", lambda e, vtt=vtt: e.tensor_tensor(
                vtt, vtt, gam[:], ALU.mult),
                waits=[t_t1, t_const], inc=1)
            P.emit("gpsimd", lambda e, vtt=vtt: e.tensor_tensor(
                vtt, vtt, bet[:], ALU.add), inc=1)
            t_gbs[tt] = P.tok("gpsimd")

        def s8_ln1(tt):
            xt = xres[:, tt * D:(tt + 1) * D]
            vtt = vt[:, tt * D:(tt + 1) * D]
            P.emit("vector", lambda e, xt=xt, vtt=vtt: e.tensor_tensor(
                xt, xt, vtt, ALU.add),
                waits=[t_gbs[tt]], inc=1)
            t_x1t = P.tok("vector")
            c1 = 16 + (tt % 2) * 8
            t_rstd1 = ln_stats(xt, c1, t_x1t, 8 + tt)
            P.emit("vector", lambda e, xt=xt, vtt=vtt, c1=c1: e.tensor_scalar(
                vtt, xt, stt[:, c1 + 2:c1 + 3], stt[:, c1 + 4:c1 + 5],
                ALU.subtract, ALU.mult),
                waits=[t_rstd1, P.tok("vector")], inc=1)
            tile_transpose(tt, P.tok("vector"))

        for tt in range(NTT):
            s8_ln2(tt)
            if tt >= 1:
                s8_ln1(tt - 1)
        s8_ln1(NTT - 1)
        t_x1 = P.tok("vector")
        last["fused"] = t_x1
        t_ln1fm = P.tok("vector")
        t_ln1 = t_ln1fm
        last["lnfm"] = t_ln1fm
        last["vt"] = P.tok("tensor")
        last["qk"] = P.tok("vector")
        dbg_tap("ln1fm", lnfm, 8192, c)
        maybe_trunc("s9")

        for tt in range(NTT):
            pg = fb[tt % 2]
            pgn = "f%d" % (tt % 2)
            for dc in range(8):
                P.emit("tensor", lambda e, pg=pg, tt=tt, dc=dc: e.matmul(
                    pg[:, 0:E], lnfm[:, dc * 1024 + tt * 128: dc * 1024 + tt * 128 + 128],
                    wg_sb[:, dc * 8:dc * 8 + 8], start=(dc == 0), stop=False),
                    waits=[t_ln1fm, tk(pgn), t_smallcast] if dc == 0 else ())
            P.emit("tensor", lambda e, pg=pg: e.matmul(
                pg[:, 0:E], onesk1(), rowc[0:1, R_BG:R_BG + E], start=False, stop=True), inc=1)
            t_g = P.tok("tensor")
            P.emit("scalar", lambda e, pg=pg, tt=tt: e.activation(
                gst[:, tt * 8:tt * 8 + 8], pg[:, 0:E], AF.Copy),
                waits=[t_g, tk("gst")], inc=1)
            last[pgn] = P.tok("scalar")
        t_logits = P.tok("scalar")
        P.emit("scalar", lambda e: e.activation(gst[:, 64:128], gst[:, 0:64], AF.Exp),
               waits=[t_logits], inc=1)
        t_ge = P.tok("scalar")
        P.emit("vector", lambda e: e.tensor_reduce(
            gst[:, 128:136], gst[:, 64:128].rearrange("p (a b) -> p a b", b=8), AX.X, ALU.add),
            waits=[t_ge], inc=1)
        P.emit("vector", lambda e: e.reciprocal(gst[:, 136:144], gst[:, 128:136]),
               waits=[P.tok("vector")], inc=1)
        t_rc = P.tok("vector")
        for tt in range(NTT):
            P.emit("vector", lambda e, tt=tt: e.tensor_single_scalar(
                gst[:, 64 + tt * 8:64 + tt * 8 + 8], gst[:, 64 + tt * 8:64 + tt * 8 + 8],
                gst[:, 136 + tt:137 + tt], ALU.mult),
                waits=[t_rc], inc=1)
        t_probs = P.tok("vector")
        P.emit("vector", lambda e: e.tensor_reduce(
            gst[:, 144:152], gst[:, 64:128].rearrange("p (a b) -> p a b", b=8), AX.X, ALU.max),
            waits=[t_probs], inc=1)
        for tt in range(NTT):
            P.emit("vector", lambda e, tt=tt: e.tensor_single_scalar(
                gst[:, 160:168], gst[:, 64 + tt * 8:64 + tt * 8 + 8],
                gst[:, 144 + tt:145 + tt], ALU.is_equal),
                waits=[P.tok("vector")], inc=1)
            P.emit("vector", lambda e: e.tensor_tensor(
                gst[:, 160:168], gst[:, 160:168], iot[:, 8:16], ALU.mult),
                waits=[P.tok("vector")], inc=1)
            P.emit("vector", lambda e: e.tensor_single_scalar(
                gst[:, 160:168], gst[:, 160:168], 8.0, ALU.add),
                waits=[P.tok("vector")], inc=1)
            P.emit("vector", lambda e, tt=tt: e.tensor_reduce(
                gst[:, 152 + tt:153 + tt], gst[:, 160:168], AX.X, ALU.min),
                waits=[P.tok("vector")], inc=1)
            P.emit("vector", lambda e, tt=tt: e.tensor_single_scalar(
                gst[:, 160:168], iot[:, 0:8], gst[:, 152 + tt:153 + tt], ALU.is_equal),
                waits=[P.tok("vector")], inc=1)
            P.emit("vector", lambda e, tt=tt: e.tensor_tensor(
                gst[:, 192 + tt * 8:192 + tt * 8 + 8], gst[:, 160:168],
                gst[:, 64 + tt * 8:64 + tt * 8 + 8], ALU.mult),
                waits=[P.tok("vector")], inc=1)
        t_coef = P.tok("vector")
        last["gst"] = t_coef
        dbg_tap("gates", gst, 256, c)
        maybe_trunc("gate")

        t_w1pre = t_w1l
        t_w2pre = t_w2l
        for ex in range(E):
            t_w1_cur = t_w1pre
            t_w2_cur = t_w2pre
            gi = 0
            for hc in range(NHC):
                b0 = (gi % 2) * 2
                pa, pc = fb[b0], fb[b0 + 1]
                pan, pcn = "f%d" % b0, "f%d" % (b0 + 1)
                wl = ([t_w1_cur, t_ln1fm, tk(pan), tk(pcn)] if gi < 2
                      else [tk(pan), tk(pcn)])
                for dc in range(8):
                    P.emit("tensor", lambda e, pa=pa, dc=dc, hc=hc: e.matmul(
                        pa[:],
                        wbuf[:, dc * 2048 + hc * 128: dc * 2048 + hc * 128 + 128],
                        lnfm[:, dc * 1024: dc * 1024 + 512],
                        start=(dc == 0), stop=(dc == 7)),
                        waits=wl if dc == 0 else ())
                    P.emit("tensor", lambda e, pc=pc, dc=dc, hc=hc: e.matmul(
                        pc[:],
                        wbuf[:, dc * 2048 + hc * 128: dc * 2048 + hc * 128 + 128],
                        lnfm[:, dc * 1024 + 512: dc * 1024 + 1024],
                        start=(dc == 0), stop=(dc == 7)),
                        inc=(1 if dc == 7 else 0))
                t_mm = P.tok("tensor")
                P.emit("scalar", lambda e, pa=pa, hc=hc, ex=ex: e.activation(
                    qk_h[:, hc * 1024: hc * 1024 + 512], pa[:],
                    AF.Gelu, bias=b1c[:, ex * 16 + hc: ex * 16 + hc + 1]),
                    waits=[t_mm, tk("qk")], inc=1)
                last[pan] = P.tok("scalar")
                P.emit("scalar", lambda e, pc=pc, hc=hc, ex=ex: e.activation(
                    qk_h[:, hc * 1024 + 512: hc * 1024 + 1024], pc[:],
                    AF.Gelu, bias=b1c[:, ex * 16 + hc: ex * 16 + hc + 1]),
                    inc=1)
                last[pcn] = P.tok("scalar")
                gi += 1
            t_h = P.tok("scalar")
            t_hmm = P.tok("tensor")
            if ex + 1 < E:
                P.emit("sync", lambda e, ex=ex: e.dma_start(
                    out=wbuf[:, 0:16384].rearrange("p (n k) -> p n k", k=H),
                    in_=w1s[ex + 1].rearrange("(n p) k -> p n k", p=128)),
                    waits=[t_w1c[ex + 1], t_hmm], inc=16)
                t_w1pre = P.tok("sync")
            ybpairs = [(4, 5), (0, 1)]
            gi = 0
            for tt in range(NTT):
                b0, b1_ = ybpairs[gi % 2]
                pa, pc = fb[b0], fb[b1_]
                pan, pcn = "f%d" % b0, "f%d" % b1_
                wl = ([t_h, t_w2_cur, tk(pan), tk(pcn)] if gi < 2
                      else [tk(pan), tk(pcn)])
                gi += 1
                for hc in range(NHC):
                    P.emit("tensor", lambda e, pa=pa, hc=hc, tt=tt: e.matmul(
                        pa[:],
                        qk_h[:, hc * 1024 + tt * 128: hc * 1024 + tt * 128 + 128],
                        wbuf[:, 16384 + hc * 1024: 16384 + hc * 1024 + 512],
                        start=(hc == 0), stop=False),
                        waits=wl if hc == 0 else ())
                    P.emit("tensor", lambda e, pc=pc, hc=hc, tt=tt: e.matmul(
                        pc[:],
                        qk_h[:, hc * 1024 + tt * 128: hc * 1024 + tt * 128 + 128],
                        wbuf[:, 16384 + hc * 1024 + 512: 16384 + hc * 1024 + 1024],
                        start=(hc == 0), stop=False))
                P.emit("tensor", lambda e, pa=pa: e.matmul(
                    pa[:], onesk1(), rows2[0:1, 0:512],
                    start=False, stop=True))
                P.emit("tensor", lambda e, pc=pc: e.matmul(
                    pc[:], onesk1(), rows2[0:1, 512:1024],
                    start=False, stop=True), inc=1)
                t_mm = P.tok("tensor")
                for dh, pb2, pbn2 in [(0, pa, pan), (1, pc, pcn)]:
                    P.emit("vector", lambda e, pb2=pb2, tt=tt, dh=dh, ex=ex: e.scalar_tensor_tensor(
                        xres[:, tt * 1024 + dh * 512: tt * 1024 + dh * 512 + 512], pb2[:],
                        gst[:, 192 + tt * 8 + ex: 192 + tt * 8 + ex + 1],
                        xres[:, tt * 1024 + dh * 512: tt * 1024 + dh * 512 + 512],
                        ALU.mult, ALU.add),
                        waits=([t_mm, t_coef] if dh == 0 else ()), inc=1)
                    last[pbn2] = P.tok("vector")
            t_ymm = P.tok("tensor")
            last["qk"] = t_ymm
            last["rows2"] = t_ymm
            if ex + 1 < E:
                P.emit("sync", lambda e, ex=ex: e.dma_start(
                    out=wbuf[:, 16384:32768].rearrange("p (n k) -> p n k", k=D),
                    in_=w2s[ex + 1, 0:H, :].rearrange("(n p) k -> p n k", p=128)),
                    waits=[t_w2c[ex + 1], t_ymm], inc=16)
                P.emit("sync", lambda e, ex=ex: e.dma_start(
                    out=rows2[0:1, 0:D], in_=w2s[ex + 1, H:H + 1, :]),
                    waits=[t_ymm], inc=16)
                t_w2pre = P.tok("sync")
        last["wb1"] = P.tok("tensor")
        last["wb2"] = P.tok("tensor")
        last["wb3"] = P.tok("tensor")
        t_ffn = P.tok("vector")
        last["gst"] = t_ffn

        P.emit("sync", lambda e, c=c: e.dma_start(
            out=out_h[c].rearrange("(a p) d -> p a d", p=128),
            in_=xres[:].rearrange("p (a d) -> p a d", d=D)),
            waits=[t_ffn], inc=16)
        last["xres"] = P.tok("sync")

    try:
        for c in range(n_chunks):
            emit_chunk(c)
    except _Trunc:
        pass

    P.emit("sync", lambda e: e.nop(), waits=[P.tok("sync")])

    P.build()
    ctx.close()
    return nc


def _host_prep(inputs):
    asc = np.ascontiguousarray
    f32 = np.float32
    bf = ml_dtypes.bfloat16
    d = {}
    d["pt_l"] = asc(np.asarray(inputs["L0_attn_proj_local"], dtype=f32).T)
    d["pt_g"] = asc(np.asarray(inputs["L0_attn_proj_global"], dtype=f32).T)
    d["wint_l"] = asc(np.asarray(inputs["L0_attn_Wl_in"], dtype=f32).T)
    d["wint_g"] = asc(np.asarray(inputs["L0_attn_Wg_in"], dtype=f32).T)
    d["wot_l"] = asc(np.asarray(inputs["L0_attn_Wl_out"], dtype=f32).T)
    d["wot_g"] = asc(np.asarray(inputs["L0_attn_Wg_out"], dtype=f32).T)
    d["w1"] = asc(np.asarray(inputs["L0_ffn_W1"], dtype=f32))
    d["w2"] = asc(np.asarray(inputs["L0_ffn_W2"], dtype=f32))
    d["b2"] = asc(np.asarray(inputs["L0_ffn_b2"], dtype=f32))
    for name, key in [("l", "L0_attn_bl_in"), ("g", "L0_attn_bg_in")]:
        b = np.asarray(inputs[key], dtype=f32)
        d["bqk_" + name] = asc(b[0:2048].reshape(16, 128).T)
        d["bv_" + name] = asc(b[2048:3072].reshape(1, 1024))
    d["bout_l"] = asc(np.asarray(inputs["L0_attn_bl_out"], dtype=f32).reshape(1, 1024))
    d["bout_g"] = asc(np.asarray(inputs["L0_attn_bg_out"], dtype=f32).reshape(1, 1024))
    d["gam"] = asc(np.broadcast_to(np.asarray(inputs["L0_attn_ln_gamma"], dtype=f32), (128, D)))
    d["bet"] = asc(np.broadcast_to(np.asarray(inputs["L0_attn_ln_beta"], dtype=f32), (128, D)))
    d["fus"] = asc(np.broadcast_to(np.asarray(inputs["L0_attn_fusion_w"], dtype=f32), (128, 2)))
    d["wgg"] = asc(np.asarray(inputs["L0_ffn_Wg"], dtype=f32).reshape(8, 128, 8).transpose(1, 0, 2).reshape(128, 64))
    d["bgg"] = asc(np.asarray(inputs["L0_ffn_bg"], dtype=f32).reshape(1, E))
    d["b1c"] = asc(np.asarray(inputs["L0_ffn_b1"], dtype=f32).reshape(8, 16, 128).transpose(2, 0, 1).reshape(128, 128))
    d["identc"] = np.eye(128, dtype=bf)
    st = np.zeros((128, 128), dtype=bf)
    st[:, 0:64] = 1
    sbm = np.zeros((128, 128), dtype=bf)
    sbm[:, 64:128] = 1
    d["seltc"] = st
    d["selbc"] = sbm
    io = np.zeros((128, 24), dtype=f32)
    io[:, 0:8] = np.arange(8, dtype=f32)
    io[:, 8:16] = np.arange(8, dtype=f32) - 8.0
    io[:, 16] = EPS
    d["iotac"] = io
    d["onesbc"] = np.ones((1, 640), dtype=bf)
    tril = np.tril(np.ones((128, 128), dtype=np.float64), -1) * 8.0
    d["tril8bc"] = tril.astype(bf)
    u1 = np.zeros((128, 128)); u1[64:128, 0:64] = np.eye(64)
    v1 = np.zeros((128, 128)); v1[64:128, :] = tril[0:64, :]
    u2 = np.zeros((128, 128)); u2[64:128, 64:128] = np.eye(64)
    v2 = np.zeros((128, 128)); v2[64:128, :] = tril[64:128, :]
    d["mu1c"] = u1.astype(bf)
    d["mv1c"] = v1.astype(bf)
    d["mu2c"] = u2.astype(bf)
    d["mv2c"] = v2.astype(bf)
    return d


def kernel(**inputs):
    x = np.asarray(inputs["L0_attn_x"], dtype=np.float32)
    B, S, _ = x.shape
    chunks = x.reshape(B * S // T, T, D)
    n_per = chunks.shape[0] // NCORES
    weights = _host_prep(inputs)

    nc = build(n_chunks=n_per)
    in_maps = []
    for ci in range(NCORES):
        m = dict(weights)
        m["x"] = np.ascontiguousarray(chunks[ci * n_per:(ci + 1) * n_per])
        in_maps.append(m)
    res = run_bass_kernel_spmd(nc, in_maps, core_ids=list(range(NCORES)))
    out = np.concatenate([r["out"] for r in res.results], axis=0)
    return out.reshape(B, S, D)
